# version 21
# baseline (speedup 1.0000x reference)
import numpy as np
import concourse.bass as bass
import concourse.mybir as mybir
from concourse.bass_utils import run_bass_kernel_spmd

F32 = mybir.dt.float32
BF16 = mybir.dt.bfloat16
AF = mybir.ActivationFunctionType
ALU = mybir.AluOpType
AX = mybir.AxisListType.X

D = 1024
H = 16
R = 2048
HALO = 512
RH = R + 2 * HALO
NW = 4
NJC = 13
DH = 4096
TP = 1024
NPASS = R // TP
SHIFT = -24.0
MASKV = -30000.0

N_CORES = 8
S_SEQ = 8192
B_BATCH = 2


def _band(jc):
    if jc < 4:
        return 0, 128 * (jc + 1)
    if jc < 8 or jc == 12:
        return 0, 512
    off = 128 * (jc - 8)
    return off, 512 - off


class Ctr:
    __slots__ = ("name", "eng", "n")

    def __init__(self, name, eng=None):
        self.name, self.eng, self.n = name, eng, 0


class Emitter:
    ENGS = ("t", "s", "v", "g", "y")

    def __init__(self):
        self.ops = {e: [] for e in self.ENGS}
        self.last_wait = {e: {} for e in self.ENGS}

    def op(self, eng, fn, waits=(), inc=None, inck=1, drain=False):
        wl = {}
        for tok in waits:
            if tok is None:
                continue
            c, v = tok
            if c.eng == eng or v <= 0:
                continue
            if wl.get(c, 0) < v:
                wl[c] = v
        resolved = []
        lw = self.last_wait[eng]
        for c, v in wl.items():
            if lw.get(c, 0) >= v:
                continue
            lw[c] = v
            resolved.append((c, v))
        tok = None
        if inc is not None:
            inc.n += inck
            tok = (inc, inc.n)
        self.ops[eng].append((resolved, fn, inc, inck, drain))
        return tok

    def replay(self, key, eng, sems):
        for waits, fn, inc, inck, drain in self.ops[key]:
            if drain:
                eng.drain()
            for c, v in waits:
                eng.wait_ge(sems[c.name], v)
            inst = fn(eng)
            if inc is not None:
                inst.then_inc(sems[inc.name], inck)


class Slot:
    __slots__ = ("w", "r")

    def __init__(self, seed=()):
        self.w = None
        self.r = list(seed)

    def ww(self):
        return self.r + ([self.w] if self.w else [])

    def wrote(self, tok):
        if tok is not None:
            self.w = tok
            self.r = []

    def rw(self):
        return [self.w] if self.w else []

    def read(self, tok):
        if tok is not None:
            self.r.append(tok)


def build(mode="full"):
    nc = bass.Bass("TRN2", target_bir_lowering=False)

    xh_d = nc.dram_tensor("xh", [RH, D], F32, kind="ExternalInput").ap()
    wq_d = nc.dram_tensor("wqT", [128, 8, D], F32, kind="ExternalInput").ap()
    wk_d = nc.dram_tensor("wkT", [128, 8, D], F32, kind="ExternalInput").ap()
    wv_d = nc.dram_tensor("wvT", [128, 8, D], F32, kind="ExternalInput").ap()
    wo_d = nc.dram_tensor("woT", [128, 8, D], F32, kind="ExternalInput").ap()
    w1_d = nc.dram_tensor("w1t", [32, 128, 1024], F32, kind="ExternalInput").ap()
    w2_d = nc.dram_tensor("w2t", [32, 128, 1024], F32, kind="ExternalInput").ap()
    h1_d = nc.dram_tensor("h1t", [32, 128, 4096], F32, kind="ExternalInput").ap()
    h2_d = nc.dram_tensor("h2t", [32, 128, 4096], F32, kind="ExternalInput").ap()
    w3_d = nc.dram_tensor("w3t", [8, 128, 4096], F32, kind="ExternalInput").ap()
    bq_d = nc.dram_tensor("bqr", [1, D], F32, kind="ExternalInput").ap()
    bk_d = nc.dram_tensor("bkr", [1, D], F32, kind="ExternalInput").ap()
    bv_d = nc.dram_tensor("bvr", [1, D], F32, kind="ExternalInput").ap()
    bo_d = nc.dram_tensor("bor", [1, D], F32, kind="ExternalInput").ap()
    s1_d = nc.dram_tensor("s1v", [128, 32], F32, kind="ExternalInput").ap()
    s2_d = nc.dram_tensor("s2v", [128, 32], F32, kind="ExternalInput").ap()
    sh1_d = nc.dram_tensor("sh1v", [128, 32], F32, kind="ExternalInput").ap()
    sh2_d = nc.dram_tensor("sh2v", [128, 32], F32, kind="ExternalInput").ap()
    s3_d = nc.dram_tensor("s3v", [128, 8], F32, kind="ExternalInput").ap()
    mb_d = nc.dram_tensor("maskb", [128, NW * NJC], F32, kind="ExternalInput").ap()
    tri_d = nc.dram_tensor("tri01", [128, 2, 128], F32, kind="ExternalInput").ap()
    id_d = nc.dram_tensor("ident", [128, 128], F32, kind="ExternalInput").ap()
    out_d = nc.dram_tensor("out", [R, D], F32, kind="ExternalOutput").ap()
    x2s_d = nc.dram_tensor("x2s", [R, D], F32).ap()

    em = Emitter()
    ct = Ctr("ct", "t")
    cs = Ctr("cs", "s")
    cv = Ctr("cv", "v")
    cgm = Ctr("cgm", "g")
    cp = Ctr("cp")
    ctri = Ctr("ctri")
    cwb = Ctr("cwb")
    ch = [Ctr("ch0"), Ctr("ch1")]
    cl = [Ctr("cl0"), Ctr("cl1"), Ctr("cl2")]
    cst = [Ctr("cst0"), Ctr("cst1")]
    ctrs = [ct, cs, cv, cgm, cp, ctri, cwb] + ch + cl + cst

    alloc = []

    def sb(name, shape, dtype):
        cm = nc.sbuf_tensor(name, shape, dtype)
        h = cm.__enter__()
        alloc.append(cm)
        return h

    def psum(name, shape, dtype):
        cm = nc.psum_tensor(name, shape, dtype)
        h = cm.__enter__()
        alloc.append(cm)
        return h

    def free_to(n):
        while len(alloc) > n:
            alloc.pop().__exit__(None, None, None)

    def gdma(ctr, dst, src, waits=()):
        return em.op("g", lambda g, d=dst, s=src: g.dma_start(d, s), waits=waits,
                     inc=ctr, inck=16)

    def ydma(ctr, dst, src, waits=()):
        return em.op("y", lambda y, d=dst, s=src: y.dma_start(d, s), waits=waits,
                     inc=ctr, inck=16)

    ones_bf = sb("ones_bf", [128, 512], BF16)
    ones_f = sb("ones_f", [128, 128], F32)
    ident = sb("identt", [128, 128], BF16)
    kpad = sb("kpad", [128, 128], BF16)
    consts = sb("consts", [128, 2], F32)
    bq_s = sb("bq_s", [1, D], BF16)
    bk_s = sb("bk_s", [1, D], BF16)
    mb_s = sb("mb_s", [128, NW * NJC], F32)
    bo_s = sb("bo_s", [1, D], BF16)
    bv_s = sb("bv_s", [1, D], BF16)
    s1_s = sb("s1_s", [128, 32], F32)
    s2_s = sb("s2_s", [128, 32], F32)
    sh1_s = sb("sh1_s", [128, 32], F32)
    sh2_s = sb("sh2_s", [128, 32], F32)
    s3_s = sb("s3_s", [128, 8], F32)
    EPS = consts[:, 0:1]
    INVN = consts[:, 1:2]
    persist_mark = len(alloc)

    em.op("g", lambda g: g.memset(ones_bf[:], 1.0))
    em.op("g", lambda g: g.memset(ones_f[:], 1.0))
    em.op("g", lambda g: g.memset(kpad[:], -1.0))
    em.op("g", lambda g: g.memset(consts[:, 0:1], 1e-5))
    t_gm = em.op("g", lambda g: g.memset(consts[:, 1:2], 1.0 / 1024.0), inc=cgm)

    for dst, src in [(bq_s[:], bq_d), (bk_s[:], bk_d), (mb_s[:], mb_d),
                     (bo_s[:], bo_d), (bv_s[:], bv_d), (s1_s[:], s1_d),
                     (s2_s[:], s2_d), (sh1_s[:], sh1_d), (sh2_s[:], sh2_d),
                     (s3_s[:], s3_d), (ident[:], id_d)]:
        gdma(cp, dst, src)

    xln_own = sb("xln_own", [128, 8, R], BF16)
    k_fm = sb("k_fm", [128, 8, RH], BF16)
    v_tm = sb("v_tm", [128, RH // 128, H * 65], BF16)
    vpad = sb("vpad", [128, H * 65], BF16)
    wbuf = sb("wbuf", [128, 8, D], BF16)
    attn_mark = len(alloc)

    em.op("g", lambda g: g.memset(
        v_tm[:].rearrange("p c (h u) -> p c h u", u=65)[:, :, :, 64:65], 1.0))
    em.op("g", lambda g: g.memset(
        vpad[:].rearrange("p (h u) -> p h u", u=65)[:, :, 0:64], -512.0))
    t_vones = em.op("g", lambda g: g.memset(
        vpad[:].rearrange("p (h u) -> p h u", u=65)[:, :, 64:65], 512.0), inc=cgm)

    wbuf2 = sb("wbuf2", [128, 8, D], BF16)
    xbuf = sb("xbuf", [128, 3, D], F32)
    lntmp = sb("lntmp", [128, 24], F32)
    xln_tm = sb("xln_tm", [128, 3, D], BF16)
    xln_L = sb("xln_L", [128, 8, HALO], BF16)
    xln_R = sb("xln_R", [128, 8, HALO], BF16)
    kv_mark = len(alloc)

    ps_tr = psum("ps_tr", [128, 1024], BF16)
    ps_p1 = psum("ps_p1", [128, 1024], F32)
    ps_p2 = psum("ps_p2", [128, 1024], F32)
    ps_p3 = psum("ps_p3", [128, 1024], F32)
    ps_bc = psum("ps_bc", [128, 512], F32)
    psum_mark = len(alloc)

    sl_xbuf = [Slot() for _ in range(3)]
    sl_xlntm = [Slot() for _ in range(3)]
    sl_pstr = Slot()
    sl_ln = [Slot() for _ in range(3)]
    sl_p1 = Slot()
    sl_p2 = Slot()
    sl_p3 = Slot()
    sl_bc = Slot()
    sl_xld = [Slot() for _ in range(RH // 128)]
    sl_vch = [Slot() for _ in range(RH // 128)]
    tok_kblk = [None] * 6

    gdma(cp, wbuf[:], wk_d)
    gdma(cp, wbuf2[:], wv_d)
    tok_cp = (cp, cp.n)
    tok_wk = tok_cp
    tok_wv = tok_cp
    tok_small = tok_cp

    def xln_dest(tc):
        if tc < 4:
            return xln_L[:, :, 128 * tc:128 * (tc + 1)]
        if tc >= 20:
            return xln_R[:, :, 128 * (tc - 20):128 * (tc - 19)]
        return xln_own[:, :, 128 * (tc - 4):128 * (tc - 3)]

    def xln_block(nt):
        if nt == 0:
            return xln_L
        if nt == 5:
            return xln_R
        return None

    def xln_blk_ap(nt, ic):
        if nt == 0:
            return xln_L[:, ic, :]
        if nt == 5:
            return xln_R[:, ic, :]
        return xln_own[:, ic, 512 * (nt - 1):512 * nt]

    def emit_ln_chain(dram_rows, pbuf, ln_slots, tmps, slx, xlntm_t,
                      extra_dma_waits=()):
        (NSUM, NMEAN, SUMSQ, MSQ, VAR, LNV, RSTD) = tmps[pbuf]
        sl_xb, sl_xt, sl_ps, sl_l2 = ln_slots
        sl_l = sl_l2[pbuf]
        xb = slx[:, pbuf, :]
        tokx = ydma(cl[pbuf % len(cl)] if pbuf < 3 else cl[0], xb, dram_rows,
                    waits=sl_xb[pbuf].ww() + list(extra_dma_waits))
        sl_xb[pbuf].wrote(tokx)
        tokv = em.op("v", lambda v: v.reduce_sum(NSUM, xb, axis=AX, negate=True),
                     waits=[tokx, t_gm] + sl_l.ww(), inc=cv)
        em.op("v", lambda v: v.tensor_scalar(NMEAN, NSUM, INVN, None, op0=ALU.mult),
              drain=True)
        toks = em.op("s", lambda s: s.activation(xlntm_t[:, pbuf, :], xb, AF.Square,
                                                 accum_out=SUMSQ),
                     waits=[tokx, tokv] + sl_xt[pbuf].ww(), inc=cs)
        em.op("v", lambda v: v.tensor_mul(MSQ, NMEAN, NMEAN), waits=[toks],
              drain=True)
        em.op("v", lambda v: v.tensor_scalar(VAR, SUMSQ, INVN, None, op0=ALU.mult))
        tokv2 = em.op("v", lambda v: v.tensor_sub(VAR, VAR, MSQ), inc=cv, drain=True)
        em.op("s", lambda s: s.activation(LNV, VAR, AF.Ln, bias=EPS), waits=[tokv2])
        toks2 = em.op("s", lambda s: s.activation(RSTD, LNV, AF.Exp, scale=-0.5),
                      inc=cs, drain=True)
        tokv3 = em.op("v", lambda v: v.tensor_scalar(xlntm_t[:, pbuf, :], xb, NMEAN,
                                                     RSTD, op0=ALU.add, op1=ALU.mult),
                      waits=[toks2, toks], inc=cv, drain=True)
        sl_l.wrote(tokv3)
        sl_xb[pbuf].read(tokv3)
        sl_xt[pbuf].wrote(tokv3)
        return tokv3

    def emit_ln_tr(dest_ap, dest_slot, pbuf, ln_slots, psx, xlntm_t, tokv3):
        sl_xb, sl_xt, sl_ps, sl_l2 = ln_slots
        tokt = None
        for c in range(8):
            tokt = em.op("t", lambda t, c=c: t.transpose(
                psx[:, 128 * c:128 * (c + 1)],
                xlntm_t[:, pbuf, 128 * c:128 * (c + 1)], ident[:]),
                waits=(sl_ps.ww() + [tokv3, tok_small] if c == 0 else []),
                inc=(ct if c == 7 else None))
        sl_xt[pbuf].read(tokt)
        sl_ps.wrote(tokt)
        tokev = em.op("v", lambda v: v.tensor_copy(
            dest_ap,
            psx[:].rearrange("p (c f) -> p c f", f=128)),
            waits=[tokt] + dest_slot.ww(), inc=cv)
        sl_ps.read(tokev)
        dest_slot.wrote(tokev)
        return tokev

    def emit_ln(dram_rows, dest_ap, dest_slot, pbuf, ln_slots, tmps, psx, slx,
                xlntm_t, extra_dma_waits=()):
        tokv3 = emit_ln_chain(dram_rows, pbuf, ln_slots, tmps, slx, xlntm_t,
                              extra_dma_waits)
        return emit_ln_tr(dest_ap, dest_slot, pbuf, ln_slots, psx, xlntm_t, tokv3)

    ln_slots = (sl_xbuf, sl_xlntm, sl_pstr, sl_ln)
    tmps = [tuple(lntmp[:, 8 * par + i:8 * par + i + 1] for i in range(7))
            for par in range(3)]

    ln1_pend = []
    for tc in range(RH // 128 + 2):
        if tc < RH // 128:
            tokv3 = emit_ln_chain(xh_d[128 * tc:128 * (tc + 1), :], tc % 3,
                                  ln_slots, tmps, xbuf, xln_tm)
            ln1_pend.append((tc, tokv3))
        if len(ln1_pend) < (2 if tc < RH // 128 else 1):
            continue
        tc, tokv3 = ln1_pend.pop(0)
        emit_ln_tr(xln_dest(tc), sl_xld[tc], tc % 3, ln_slots, ps_tr, xln_tm,
                   tokv3)
        xsrc = xln_dest(tc)
        tokm = None
        for ic in range(8):
            for on in range(2):
                tokm = em.op("t", lambda t, on=on, ic=ic, xsrc=xsrc: t.matmul(
                    ps_p1[:, 512 * on:512 * (on + 1)],
                    xsrc[:, ic, :],
                    wbuf2[:, ic, 512 * on:512 * (on + 1)],
                    start=(ic == 0), stop=False),
                    waits=(sl_p1.ww() + [sl_xld[tc].w, tok_wv]
                           if (on == 0 and ic == 0) else []))
        for on in range(2):
            tokm = em.op("t", lambda t, on=on: t.matmul(
                ps_p1[:, 512 * on:512 * (on + 1)], ones_bf[0:1, 0:128],
                bv_s[0:1, 512 * on:512 * (on + 1)], start=False, stop=True),
                inc=(ct if on == 1 else None))
        sl_p1.wrote(tokm)
        sl_xld[tc].read(tokm)
        toke = em.op("s", lambda s, tc=tc: s.activation(
            v_tm[:, tc, :].rearrange("p (h u) -> p h u", u=65)[:, :, 0:64],
            ps_p1[:].rearrange("p (h u) -> p h u", u=64), AF.Copy),
            waits=[tokm] + sl_vch[tc].ww(), inc=cs)
        sl_p1.read(toke)
        sl_vch[tc].wrote(toke)
        if tc % 4 == 3:
            nt = tc // 4
            blk_deps = [sl_xld[j].w for j in range(4 * nt, 4 * nt + 4)]
            toke_k = None
            for oc in range(8):
                pk = ps_p2 if oc % 2 == 0 else ps_p3
                slk = sl_p2 if oc % 2 == 0 else sl_p3
                half = (oc // 2) % 2
                tokmk = None
                for ic in range(8):
                    em.op("t", lambda t, oc=oc, ic=ic, half=half, nt=nt,
                          pk=pk: t.matmul(
                        pk[:, 512 * half:512 * (half + 1)],
                        wbuf[:, ic, 128 * oc:128 * (oc + 1)],
                        xln_blk_ap(nt, ic),
                        start=(ic == 0), stop=False),
                        waits=(slk.ww() + blk_deps + [tok_wk] if ic == 0 else []))
                tokmk = em.op("t", lambda t, oc=oc, half=half, pk=pk: t.matmul(
                    pk[:, 512 * half:512 * (half + 1)],
                    bk_s[0:1, 128 * oc:128 * (oc + 1)], ones_bf[0:1, 0:512],
                    start=False, stop=True), inc=ct)
                slk.wrote(tokmk)
                toke_k = em.op("s", lambda s, oc=oc, nt=nt, half=half, pk=pk:
                               s.activation(
                    k_fm[:, oc, 512 * nt:512 * (nt + 1)],
                    pk[:, 512 * half:512 * (half + 1)],
                    AF.Copy),
                    waits=[tokmk], inc=cs)
                slk.read(toke_k)
            tok_kblk[nt] = toke_k

    tok_kv_done = (ct, ct.n)

    free_to(attn_mark)
    tri_s = sb("tri_s", [128, 2, 128], BF16)
    q_fm = sb("q_fm", [128, 8, 512], BF16)
    probsT = sb("probsT", [128, 3, 1024], BF16)
    attn_un = sb("attn_un", [128, 512], F32)
    attn_fm = sb("attn_fm", [128, 2, 8, 512], BF16)
    rsum_sb = sb("rsum_sb", [128, 1024], F32)
    x2buf = sb("x2buf", [128, 1, D], F32)
    xbuf = sb("xbuf_w", [128, 2, D], F32)[:].tensor if False else sb("xbuf_w", [128, 1, D], F32)
    ps_scp = [psum("ps_scA", [128, 1024], F32), psum("ps_scB", [128, 1024], F32),
              psum("ps_scC", [128, 1024], F32)]
    ps_pv2 = psum("ps_pv2", [128, 1024], F32)

    bar = [(ct, ct.n), (cs, cs.n), (cv, cv.n)]
    sl_xbuf = [Slot(bar), Slot(bar)]
    sl_q = Slot(bar)
    sl_probs = [Slot(bar), Slot(bar), Slot(bar)]
    sl_attnun = [Slot(bar), Slot(bar)]
    sl_attnfm = [Slot(bar), Slot(bar)]
    sl_rsum = [Slot(bar), Slot(bar)]
    sl_x2buf = [Slot(bar), Slot(bar)]
    sl_wbuf = Slot(bar)
    sl_scp = [Slot(bar), Slot(bar), Slot(bar)]
    sl_pv = [Slot(bar), Slot(bar)]

    tok_tri = gdma(ctri, tri_s[:], tri_d,
                   waits=[tok_kv_done, (cs, cs.n), (cv, cv.n)])
    tok_w = gdma(cwb, wbuf[:], wq_d,
                 waits=[tok_kv_done, (cs, cs.n), (cv, cv.n)])
    sl_wbuf.wrote(tok_w)

    pend_tails = []
    attn_done_tok = [None, None]

    def emit_oproj_group(wo_, tc):
        wpar = wo_ % 2
        pb2 = tc % 2
        rows = xh_d[HALO + 512 * wo_ + 128 * tc: HALO + 512 * wo_ + 128 * (tc + 1), :]
        tokx = ydma(cl[pb2], xbuf[:, 0, :], rows, waits=sl_xbuf[0].ww())
        sl_xbuf[0].wrote(tokx)
        pko = ps_scp[2]
        pslo = sl_scp[2]
        tokmo = None
        for ic in range(8):
            for on in range(2):
                tokmo = em.op("t", lambda t, on=on, ic=ic, tc=tc, pko=pko,
                              wpar=wpar: t.matmul(
                    pko[:, 512 * on:512 * (on + 1)],
                    attn_fm[:, wpar, ic, 128 * tc:128 * (tc + 1)],
                    wbuf[:, ic, 512 * on:512 * (on + 1)],
                    start=(ic == 0), stop=False),
                    waits=(pslo.ww() + [attn_done_tok[wpar], sl_wbuf.w]
                           if (on == 0 and ic == 0) else []))
        for on in range(2):
            tokmo = em.op("t", lambda t, on=on, pko=pko: t.matmul(
                pko[:, 512 * on:512 * (on + 1)], ones_bf[0:1, 0:128],
                bo_s[0:1, 512 * on:512 * (on + 1)], start=False, stop=True),
                inc=(ct if on == 1 else None))
        pslo.wrote(tokmo)
        sl_wbuf.read(tokmo)
        if tc == 3:
            sl_attnfm[wpar].read(tokmo)
        toka = em.op("v", lambda v, pko=pko: v.tensor_add(
            x2buf[:, 0, :], pko[:, :], xbuf[:, 0, :]),
            waits=[tokmo, tokx] + sl_x2buf[0].ww(), inc=cv)
        pslo.read(toka)
        sl_xbuf[0].read(toka)
        sl_x2buf[0].wrote(toka)
        drows = (x2s_d if mode == "full" else out_d)[
            512 * wo_ + 128 * tc: 512 * wo_ + 128 * (tc + 1), :]
        tokst = ydma(cst[pb2], drows, x2buf[:, 0, :], waits=[toka])
        sl_x2buf[0].read(tokst)

    for w in range(NW):
        toke_q = None
        for oc in range(8):
            half = oc % 2
            pq = ps_scp[half]
            slq = sl_scp[half]
            for ic in range(8):
                em.op("t", lambda t, oc=oc, ic=ic, w=w, pq=pq: t.matmul(
                    pq[:, 0:512],
                    wbuf[:, ic, 128 * oc:128 * (oc + 1)],
                    xln_own[:, ic, 512 * w:512 * (w + 1)],
                    start=(ic == 0), stop=False),
                    waits=(slq.ww() + [sl_wbuf.w] if ic == 0 else []))
            tokmq = em.op("t", lambda t, oc=oc, pq=pq: t.matmul(
                pq[:, 0:512],
                bq_s[0:1, 128 * oc:128 * (oc + 1)], ones_bf[0:1, 0:512],
                start=False, stop=True), inc=ct)
            slq.wrote(tokmq)
            toke_q = em.op("s", lambda s, oc=oc, pq=pq: s.activation(
                q_fm[:, oc, :], pq[:, 0:512],
                AF.Copy),
                waits=[tokmq] + (sl_q.ww() if oc == 0 else []), inc=cs)
            slq.read(toke_q)
            sl_wbuf.read(tokmq)
        sl_q.wrote(toke_q)

        tok_w = gdma(cwb, wbuf[:], wo_d, waits=sl_wbuf.ww())
        sl_wbuf.wrote(tok_w)

        CORDER = [4, 5, 6, 7, 12, 0, 1, 2, 3, 8, 9, 10, 11]
        for hc in range(H // 2):
            exp_toks = [None] * 13
            sc_toks = [None] * 13

            def emit_scores(ci, pb, hc=hc, w=w, sc_toks=sc_toks):
                jc = CORDER[ci]
                pps = ps_scp[pb]
                off, wid = _band(jc)
                if jc < 12:
                    kof = 512 * w + 128 * jc
                    kdep = tok_kblk[kof // 512]
                else:
                    kdep = t_gm
                tokm = None
                for hp in range(2):
                    hoff = 64 * hp
                    if jc < 12:
                        klhs = k_fm[hoff:hoff + 64, hc, kof:kof + 128]
                    else:
                        klhs = kpad[hoff:hoff + 64, :]
                    tokm = em.op("t", lambda t, klhs=klhs, hp=hp, pps=pps,
                                 hoff=hoff, hc=hc, off=off, wid=wid: t.matmul(
                        pps[:, 512 * hp + off:512 * hp + off + wid],
                        klhs, q_fm[hoff:hoff + 64, hc, off:off + wid],
                        start=True, stop=True),
                        waits=(sl_scp[pb].ww() + [sl_q.w, kdep]) if hp == 0 else [],
                        inc=(ct if hp == 1 else None))
                sl_scp[pb].wrote(tokm)
                sl_q.read(tokm)
                sc_toks[ci] = tokm

            def emit_exp(ci, pb, w=w, sc_toks=sc_toks, exp_toks=exp_toks):
                jc = CORDER[ci]
                pps = ps_scp[pb]
                off, wid = _band(jc)
                ws_ = [sc_toks[ci], tok_small] + sl_probs[pb].ww()
                toke = None
                if wid == 512:
                    toke = em.op("s", lambda s, pb=pb, pps=pps, jc=jc, w=w:
                                 s.activation(
                        probsT[:, pb, 0:1024], pps[:, 0:1024],
                        AF.Exp, bias=mb_s[:, w * NJC + jc:w * NJC + jc + 1]),
                        waits=ws_, inc=cs)
                else:
                    for hp in range(2):
                        lo = 512 * hp + off
                        toke = em.op("s", lambda s, pb=pb, pps=pps, jc=jc, w=w,
                                     lo=lo, wid=wid: s.activation(
                            probsT[:, pb, lo:lo + wid], pps[:, lo:lo + wid],
                            AF.Exp, bias=mb_s[:, w * NJC + jc:w * NJC + jc + 1]),
                            waits=ws_, inc=cs)
                        ws_ = []
                sl_scp[pb].read(toke)
                sl_probs[pb].wrote(toke)
                ptok = toke
                if jc < 4:
                    tlo, tt = 128 * jc, 0
                elif 8 <= jc < 12:
                    tlo, tt = 128 * (jc - 8), 1
                else:
                    tlo = None
                if tlo is not None:
                    for hp in range(2):
                        lo = 512 * hp + tlo
                        ptok = em.op("v", lambda v, pb=pb, lo=lo, tt=tt:
                                     v.tensor_mul(
                            probsT[:, pb, lo:lo + 128], probsT[:, pb, lo:lo + 128],
                            tri_s[:, tt, :]),
                            waits=[toke, tok_tri], inc=cv)
                    sl_probs[pb].wrote(ptok)
                exp_toks[ci] = ptok

            def emit_pv(ci, pb, first_pv, last_pv, hc=hc, w=w,
                        exp_toks=exp_toks):
                jc = CORDER[ci]
                off, wid = _band(jc)
                tokpv = None
                for hp in range(2):
                    h = 2 * hc + hp
                    if jc < 12:
                        vl = v_tm[:, 4 * w + jc, :].rearrange(
                            "p (h u) -> p h u", u=65)[:, h, :]
                        vdep = sl_vch[4 * w + jc].w
                    else:
                        vl = vpad[:].rearrange("p (h u) -> p h u", u=65)[:, h, :]
                        vdep = t_vones
                    pvlo = 512 * hp
                    tokpv = em.op("t", lambda t, vl=vl, pb=pb, hp=hp, pvlo=pvlo,
                                  off=off, wid=wid, st_=first_pv, sp_=last_pv:
                                  t.matmul(
                        ps_pv2[0:65, pvlo + off:pvlo + off + wid],
                        vl, probsT[:, pb, 512 * hp + off:512 * hp + off + wid],
                        start=st_, stop=sp_),
                        waits=[exp_toks[ci], vdep]
                        + (sl_pv[hp].ww() if first_pv else []),
                        inc=(ct if (last_pv and hp == 1) else None))
                return tokpv

            for i0 in range(3):
                emit_scores(i0, i0 % 3)
                emit_exp(i0, i0 % 3)
            if w >= 1 and hc in (1, 3, 5, 7):
                emit_oproj_group(w - 1, (hc - 1) // 2)
            while pend_tails:
                pend_tails.pop(0)()
            for i in range(13):
                tokpv = emit_pv(i, i % 3, first_pv=(i == 0), last_pv=(i == 12))
                if i + 3 < 13:
                    emit_scores(i + 3, (i + 3) % 3)
                    emit_exp(i + 3, (i + 3) % 3)
            sl_pv[0].wrote(tokpv)
            sl_pv[1].wrote(tokpv)
            for sp in sl_probs:
                sp.read(tokpv)
            for hp in range(2):
                pvlo = 512 * hp
                hoff = 64 * hp
                tokr = em.op("v", lambda v, pvlo=pvlo: v.reciprocal(
                    rsum_sb[64:65, pvlo:pvlo + 512],
                    ps_pv2[64:65, pvlo:pvlo + 512]),
                    waits=[tokpv] + sl_rsum[hp].ww(), inc=cv)
                sl_rsum[hp].wrote(tokr)
                toke2 = em.op("s", lambda s, pvlo=pvlo, hoff=hoff: s.activation(
                    attn_un[hoff:hoff + 64, :], ps_pv2[0:64, pvlo:pvlo + 512],
                    AF.Copy),
                    waits=[tokpv] + sl_attnun[hp].ww(), inc=cs)
                sl_attnun[hp].wrote(toke2)
                sl_pv[hp].read(tokr)
                sl_pv[hp].read(toke2)

                def mk_tail(hp=hp, tokr=tokr, toke2=toke2, hc=hc, wpar=w % 2):
                    def run():
                        hoff = 64 * hp
                        pvlo = 512 * hp
                        tokb = em.op("t", lambda t: t.matmul(
                            ps_pv2[:, pvlo:pvlo + 512], ones_f[64:65, :],
                            rsum_sb[64:65, pvlo:pvlo + 512],
                            start=True, stop=True),
                            waits=[tokr, toke2, t_gm] + sl_pv[hp].ww(), inc=ct)
                        sl_rsum[hp].read(tokb)
                        sl_pv[hp].wrote(tokb)
                        tokn = em.op("v", lambda v: v.tensor_mul(
                            attn_fm[hoff:hoff + 64, wpar, hc, :],
                            attn_un[hoff:hoff + 64, :],
                            ps_pv2[hoff:hoff + 64, pvlo:pvlo + 512]),
                            waits=[tokb, toke2]
                            + (sl_attnfm[wpar].ww() if hc == 0 else []), inc=cv)
                        sl_attnun[hp].read(tokn)
                        sl_pv[hp].read(tokn)
                        if hc == H // 2 - 1 and hp == 1:
                            sl_attnfm[wpar].wrote(tokn)
                            attn_done_tok[wpar] = tokn
                    return run
                pend_tails.append(mk_tail())
        if w < NW - 1:
            tok_w = gdma(cwb, wbuf[:], wq_d, waits=sl_wbuf.ww())
            sl_wbuf.wrote(tok_w)

    while pend_tails:
        pend_tails.pop(0)()
    for tc in range(4):
        emit_oproj_group(NW - 1, tc)

    if mode == "attn":
        _finalize(nc, em, ctrs, [(cst[0], cst[0].n), (cst[1], cst[1].n)])
        return nc

    free_to(persist_mark)
    bar = [(ct, ct.n), (cs, cs.n), (cv, cv.n),
           (cst[0], cst[0].n), (cst[1], cst[1].n),
           (cl[0], cl[0].n), (cl[1], cl[1].n), (cwb, cwb.n), (ctri, ctri.n)]

    ps_tr2 = psum("ps_tr2", [128, 1024], BF16)
    ps_fa = psum("ps_fa", [128, 1024], F32)
    ps_fb = psum("ps_fb", [128, 1024], F32)

    x2in = sb("x2in", [128, 2, D], F32)
    lntmp2 = sb("lntmp2", [128, 16], F32)
    xln2_tm = sb("xln2_tm", [128, 2, D], BF16)
    xln2_fm = sb("xln2_fm", [128, 8, TP], BF16)
    y_fm = sb("y_fm", [128, 8, TP], BF16)
    a_fm = sb("a_fm", [128, 32, TP], BF16)
    b_fm = sb("b_fm", [128, 32, TP], BF16)
    hcol = sb("hcol", [128, 2, 4096], BF16)
    t1_sb = sb("t1_sb", [128, TP], BF16)
    osb = sb("osb", [128, 1, D], F32)

    tmps2 = [tuple(lntmp2[:, 8 * par + i:8 * par + i + 1] for i in range(7))
             for par in range(2)]

    sl_x2in = [Slot(bar), Slot(bar)]
    sl_x2tm = [Slot(bar), Slot(bar)]
    sl_ln2 = [Slot(bar), Slot(bar)]
    sl_ps2 = Slot(bar)
    sl_xln2 = Slot(bar)
    sl_afm = Slot(bar)
    sl_bfm = Slot(bar)
    sl_hcol = [Slot(bar), Slot(bar)]
    sl_t1 = Slot(bar)
    sl_fa = Slot(bar)
    sl_fb = Slot(bar)
    sl_osb = [Slot(bar), Slot(bar)]
    sl_yfm = Slot(bar)
    ln2_slots = (sl_x2in, sl_x2tm, sl_ps2, sl_ln2)

    x2_stores = [(cst[0], cst[0].n), (cst[1], cst[1].n)]

    def emit_ln2(p):
        tokev = None
        pend = []
        for tc in range(TP // 128 + 2):
            if tc < TP // 128:
                row0 = TP * p + 128 * tc
                tokv3 = emit_ln_chain(x2s_d[row0:row0 + 128, :], tc % 2,
                                      ln2_slots, tmps2, x2in, xln2_tm,
                                      extra_dma_waits=x2_stores)
                pend.append((tc, tokv3))
            if len(pend) < (2 if tc < TP // 128 else 1):
                continue
            tcp, tokv3 = pend.pop(0)
            tokev = emit_ln_tr(xln2_fm[:, :, 128 * tcp:128 * (tcp + 1)],
                               sl_xln2 if tcp == TP // 128 - 1 else Slot(),
                               tcp % 2, ln2_slots, ps_tr2, xln2_tm, tokv3)
        sl_xln2.wrote(tokev)

    emit_ln2(0)
    for p in range(NPASS):
        tokav = None
        for oc in range(32):
            hb = oc % 2
            tokw1 = gdma(ch[hb], hcol[:, hb, 0:1024], w1_d[oc],
                         waits=sl_hcol[hb].ww())
            tokw2 = gdma(ch[hb], hcol[:, hb, 1024:2048], w2_d[oc])
            sl_hcol[hb].wrote(tokw2)
            tokm1 = None
            for nt in range(TP // 512):
                for ic in range(8):
                    tokm1 = em.op("t", lambda t, ic=ic, nt=nt, hb=hb: t.matmul(
                        ps_fa[:, 512 * nt:512 * (nt + 1)],
                        hcol[:, hb, 128 * ic:128 * (ic + 1)],
                        xln2_fm[:, ic, 512 * nt:512 * (nt + 1)],
                        start=(ic == 0), stop=(ic == 7)),
                        waits=(sl_fa.ww() + [tokw2, sl_xln2.w]
                               if (nt == 0 and ic == 0) else []),
                        inc=(ct if (nt == TP // 512 - 1 and ic == 7) else None))
            sl_fa.wrote(tokm1)
            tokm2 = None
            for nt in range(TP // 512):
                for ic in range(8):
                    tokm2 = em.op("t", lambda t, ic=ic, nt=nt, hb=hb: t.matmul(
                        ps_fb[:, 512 * nt:512 * (nt + 1)],
                        hcol[:, hb, 1024 + 128 * ic:1024 + 128 * (ic + 1)],
                        xln2_fm[:, ic, 512 * nt:512 * (nt + 1)],
                        start=(ic == 0), stop=(ic == 7)),
                        waits=(sl_fb.ww() if (nt == 0 and ic == 0) else []),
                        inc=(ct if (nt == TP // 512 - 1 and ic == 7) else None))
            sl_fb.wrote(tokm2)
            sl_hcol[hb].read(tokm2)
            sl_xln2.read(tokm2)
            toksl = em.op("s", lambda s, oc=oc: s.activation(
                t1_sb[:], ps_fa[:, :], AF.Silu, scale=s1_s[:, oc:oc + 1]),
                waits=[tokm1] + sl_t1.ww(), inc=cs)
            sl_fa.read(toksl)
            sl_t1.wrote(toksl)
            tokav = em.op("v", lambda v, oc=oc: v.scalar_tensor_tensor(
                a_fm[:, oc, :], ps_fb[:, :], s2_s[:, oc:oc + 1], t1_sb[:],
                op0=ALU.mult, op1=ALU.mult),
                waits=[tokm2, toksl] + (sl_afm.ww() if oc == 0 else []), inc=cv)
            sl_fb.read(tokav)
            sl_t1.read(tokav)
        sl_afm.wrote(tokav)

        def big_stage(src_fm, src_slot, dst_fm, dst_slot, w_dram, sc_ap, nob, act,
                      nic, filler=None):
            tokl = None
            for oc in range(nob):
                if filler is not None:
                    filler(oc)
                hb = oc % 2
                pso = ps_fa if oc % 2 == 0 else ps_fb
                slo = sl_fa if oc % 2 == 0 else sl_fb
                tokw = gdma(ch[hb], hcol[:, hb, 0:128 * nic],
                            w_dram[oc][:, 0:128 * nic], waits=sl_hcol[hb].ww())
                sl_hcol[hb].wrote(tokw)
                tokm = None
                for nt in range(TP // 512):
                    for ic in range(nic):
                        tokm = em.op("t", lambda t, ic=ic, nt=nt, hb=hb, pso=pso:
                                     t.matmul(
                            pso[:, 512 * nt:512 * (nt + 1)],
                            hcol[:, hb, 128 * ic:128 * (ic + 1)],
                            src_fm[:, ic, 512 * nt:512 * (nt + 1)],
                            start=(ic == 0), stop=(ic == nic - 1)),
                            waits=(slo.ww() + [tokw, src_slot.w]
                                   if (nt == 0 and ic == 0) else []),
                            inc=(ct if (nt == TP // 512 - 1 and ic == nic - 1)
                                 else None))
                slo.wrote(tokm)
                sl_hcol[hb].read(tokm)
                src_slot.read(tokm)
                tokl = em.op("s", lambda s, oc=oc, pso=pso: s.activation(
                    dst_fm[:, oc, :], pso[:, :], act, scale=sc_ap[:, oc:oc + 1]),
                    waits=[tokm] + (dst_slot.ww() if oc == 0 else []), inc=cs)
                slo.read(tokl)
            dst_slot.wrote(tokl)
            return tokl

        if p + 1 < NPASS:
            ln2_state = [0, None]

            def ln2_filler(oc, p=p):
                tc_n, pend = ln2_state
                if oc % 2 == 1 and pend is not None:
                    tc, tokv3 = pend
                    tokev = emit_ln_tr(xln2_fm[:, :, 128 * tc:128 * (tc + 1)],
                                       sl_xln2 if tc == TP // 128 - 1 else Slot(),
                                       tc % 2, ln2_slots, ps_tr2, xln2_tm, tokv3)
                    if tc == TP // 128 - 1:
                        sl_xln2.wrote(tokev)
                    ln2_state[1] = None
                elif oc % 2 == 0 and tc_n < TP // 128 and pend is None:
                    row0 = TP * (p + 1) + 128 * tc_n
                    tokv3 = emit_ln_chain(x2s_d[row0:row0 + 128, :], tc_n % 2,
                                          ln2_slots, tmps2, x2in, xln2_tm,
                                          extra_dma_waits=x2_stores)
                    ln2_state[0] = tc_n + 1
                    ln2_state[1] = (tc_n, tokv3)
        else:
            ln2_filler = None
        big_stage(a_fm, sl_afm, b_fm, sl_bfm, h1_d, sh1_s, 32, AF.Gelu, 32,
                  filler=ln2_filler)
        big_stage(b_fm, sl_bfm, a_fm, sl_afm, h2_d, sh2_s, 32, AF.Gelu, 32)
        toky = big_stage(a_fm, sl_afm, y_fm, sl_yfm, w3_d, s3_s, 8, AF.Copy, 32)

        for tc in range(TP // 128):
            row0 = TP * p + 128 * tc
            pb2 = tc % 2
            tokx = ydma(cl[pb2], x2in[:, pb2, :], x2s_d[row0:row0 + 128, :],
                        waits=sl_x2in[pb2].ww() + x2_stores)
            sl_x2in[pb2].wrote(tokx)
            tokt = None
            for c in range(8):
                tokt = em.op("t", lambda t, c=c, tc=tc: t.transpose(
                    ps_tr2[:, 128 * c:128 * (c + 1)],
                    y_fm[:, c, 128 * tc:128 * (tc + 1)], ident[:]),
                    waits=(sl_ps2.ww() + [toky] if c == 0 else []),
                    inc=(ct if c == 7 else None))
            sl_ps2.wrote(tokt)
            sl_yfm.read(tokt)
            toko = em.op("v", lambda v, pb2=pb2: v.tensor_add(
                osb[:, 0, :], ps_tr2[:, :], x2in[:, pb2, :]),
                waits=[tokt, tokx] + sl_osb[0].ww(), inc=cv)
            sl_ps2.read(toko)
            sl_x2in[pb2].read(toko)
            sl_osb[0].wrote(toko)
            tokst = ydma(cst[pb2], out_d[row0:row0 + 128, :], osb[:, 0, :],
                         waits=[toko])
            sl_osb[0].read(tokst)

    _finalize(nc, em, ctrs, [(cst[0], cst[0].n), (cst[1], cst[1].n)])
    return nc


def _finalize(nc, em, ctrs, final_tok):
    import contextlib
    with contextlib.ExitStack() as st:
        sems = {c.name: st.enter_context(nc.semaphore(c.name)) for c in ctrs}
        with nc.Block() as block:

            @block.tensor
            def _(eng):
                em.replay("t", eng, sems)

            @block.scalar
            def _(eng):
                em.replay("s", eng, sems)

            @block.vector
            def _(eng):
                em.replay("v", eng, sems)

            @block.gpsimd
            def _(eng):
                em.replay("g", eng, sems)

            @block.sync
            def _(eng):
                em.replay("y", eng, sems)
                for c, v in final_tok:
                    eng.wait_ge(sems[c.name], v)


def _prep_shared(L0_attn_wq, L0_attn_bq, L0_attn_wk, L0_attn_bk, L0_attn_wv,
                 L0_attn_bv, L0_attn_wo, L0_attn_bo, L0_ffn_w1_w, L0_ffn_w1_s,
                 L0_ffn_w2_w, L0_ffn_w2_s, L0_ffn_h1_w, L0_ffn_h1_s, L0_ffn_h2_w,
                 L0_ffn_h2_s, L0_ffn_w3_w, L0_ffn_w3_s):
    f = np.float32
    c = np.ascontiguousarray

    def wt_qkvo(w):
        return c(np.asarray(w, f).T.reshape(8, 128, D).transpose(1, 0, 2))

    def tile_in(wT, nic, noc):
        t = np.asarray(wT, f).reshape(nic, 128, noc, 128).transpose(2, 1, 0, 3)
        return c(t.reshape(noc, 128, nic * 128))

    d = {
        "wqT": wt_qkvo(L0_attn_wq), "wkT": wt_qkvo(L0_attn_wk),
        "wvT": wt_qkvo(L0_attn_wv), "woT": wt_qkvo(L0_attn_wo),
        "w1t": tile_in(np.asarray(L0_ffn_w1_w, f).T, 8, 32),
        "w2t": tile_in(np.asarray(L0_ffn_w2_w, f).T, 8, 32),
        "h1t": tile_in(np.asarray(L0_ffn_h1_w, f).T, 32, 32),
        "h2t": tile_in(np.asarray(L0_ffn_h2_w, f).T, 32, 32),
        "w3t": tile_in(np.asarray(L0_ffn_w3_w, f).T, 32, 8),
        "bqr": np.asarray(L0_attn_bq, f).reshape(1, D).copy(),
        "bkr": np.asarray(L0_attn_bk, f).reshape(1, D).copy(),
        "bvr": np.asarray(L0_attn_bv, f).reshape(1, D).copy(),
        "bor": np.asarray(L0_attn_bo, f).reshape(1, D).copy(),
        "s1v": c(np.asarray(L0_ffn_w1_s, f).reshape(32, 128).T),
        "s2v": c(np.asarray(L0_ffn_w2_s, f).reshape(32, 128).T),
        "sh1v": c(np.asarray(L0_ffn_h1_s, f).reshape(32, 128).T),
        "sh2v": c(np.asarray(L0_ffn_h2_s, f).reshape(32, 128).T),
        "s3v": c(np.asarray(L0_ffn_w3_s, f).reshape(8, 128).T),
        "ident": np.eye(128, dtype=f),
    }
    pp = np.arange(128)[:, None]
    ii = np.arange(128)[None, :]
    tri = np.zeros((128, 2, 128), f)
    tri[:, 0, :] = (ii <= pp)
    tri[:, 1, :] = (ii >= pp)
    d["tri01"] = tri
    return d


def _prep_core(x, core):
    f = np.float32
    b = core // 4
    T0 = (core % 4) * R
    xh = np.zeros((RH, D), f)
    lo = max(0, T0 - HALO)
    hi = min(S_SEQ, T0 + R + HALO)
    xh[lo - (T0 - HALO): hi - (T0 - HALO), :] = np.asarray(x[b, lo:hi, :], f)
    mb = np.empty((128, NW * NJC), f)
    for w in range(NW):
        for jc in range(12):
            g = T0 - HALO + 512 * w + 128 * jc + np.arange(128)
            real = (g >= 0) & (g < S_SEQ)
            mb[:, w * NJC + jc] = np.where(real, SHIFT, SHIFT + MASKV)
        edge = (T0 + 512 * w == 0) or (T0 + 512 * w + 512 == S_SEQ)
        col = np.full(128, SHIFT + MASKV, f)
        if edge:
            col[0] = SHIFT
        mb[:, w * NJC + 12] = col
    return {"xh": xh, "maskb": mb}


def kernel(**inputs):
    mode = inputs.pop("_mode", "full")
    x = np.asarray(inputs.pop("L0_attn_x"), np.float32)
    shared = _prep_shared(**inputs)
    in_maps = []
    for core in range(N_CORES):
        m = dict(shared)
        m.update(_prep_core(x, core))
        in_maps.append(m)
    nc = build(mode)
    res = run_bass_kernel_spmd(nc, in_maps, core_ids=list(range(N_CORES)))
    parts = [res.results[core]["out"] for core in range(N_CORES)]
    out = np.empty((B_BATCH, S_SEQ, D), np.float32)
    for core in range(N_CORES):
        b = core // 4
        T0 = (core % 4) * R
        out[b, T0:T0 + R, :] = parts[core]
    return out


# version 22
# speedup vs baseline: 1.0428x; 1.0428x over previous
import numpy as np
import concourse.bass as bass
import concourse.mybir as mybir
from concourse.bass_utils import run_bass_kernel_spmd

F32 = mybir.dt.float32
BF16 = mybir.dt.bfloat16
AF = mybir.ActivationFunctionType
ALU = mybir.AluOpType
AX = mybir.AxisListType.X

D = 1024
H = 16
R = 2048
HALO = 512
RH = R + 2 * HALO
NW = 4
NJC = 13
DH = 4096
TP = 1024
NPASS = R // TP
SHIFT = -24.0
MASKV = -30000.0

N_CORES = 8
S_SEQ = 8192
B_BATCH = 2


def _band(jc):
    if jc < 4:
        return 0, 128 * (jc + 1)
    if jc < 8 or jc == 12:
        return 0, 512
    off = 128 * (jc - 8)
    return off, 512 - off


class Ctr:
    __slots__ = ("name", "eng", "n")

    def __init__(self, name, eng=None):
        self.name, self.eng, self.n = name, eng, 0


class Emitter:
    ENGS = ("t", "s", "v", "g", "y")

    def __init__(self):
        self.ops = {e: [] for e in self.ENGS}
        self.last_wait = {e: {} for e in self.ENGS}

    def op(self, eng, fn, waits=(), inc=None, inck=1, drain=False):
        wl = {}
        for tok in waits:
            if tok is None:
                continue
            c, v = tok
            if c.eng == eng or v <= 0:
                continue
            if wl.get(c, 0) < v:
                wl[c] = v
        resolved = []
        lw = self.last_wait[eng]
        for c, v in wl.items():
            if lw.get(c, 0) >= v:
                continue
            lw[c] = v
            resolved.append((c, v))
        tok = None
        if inc is not None:
            inc.n += inck
            tok = (inc, inc.n)
        self.ops[eng].append((resolved, fn, inc, inck, drain))
        return tok

    def replay(self, key, eng, sems):
        for waits, fn, inc, inck, drain in self.ops[key]:
            if drain:
                eng.drain()
            for c, v in waits:
                eng.wait_ge(sems[c.name], v)
            inst = fn(eng)
            if inc is not None:
                inst.then_inc(sems[inc.name], inck)


class Slot:
    __slots__ = ("w", "r")

    def __init__(self, seed=()):
        self.w = None
        self.r = list(seed)

    def ww(self):
        return self.r + ([self.w] if self.w else [])

    def wrote(self, tok):
        if tok is not None:
            self.w = tok
            self.r = []

    def rw(self):
        return [self.w] if self.w else []

    def read(self, tok):
        if tok is not None:
            self.r.append(tok)


def build(mode="full"):
    nc = bass.Bass("TRN2", target_bir_lowering=False)

    xh_d = nc.dram_tensor("xh", [RH, D], F32, kind="ExternalInput").ap()
    wq_d = nc.dram_tensor("wqT", [128, 8, D], F32, kind="ExternalInput").ap()
    wk_d = nc.dram_tensor("wkT", [128, 8, D], F32, kind="ExternalInput").ap()
    wv_d = nc.dram_tensor("wvT", [128, 8, D], F32, kind="ExternalInput").ap()
    wo_d = nc.dram_tensor("woT", [128, 8, D], F32, kind="ExternalInput").ap()
    w1_d = nc.dram_tensor("w1t", [32, 128, 1024], F32, kind="ExternalInput").ap()
    w2_d = nc.dram_tensor("w2t", [32, 128, 1024], F32, kind="ExternalInput").ap()
    h1_d = nc.dram_tensor("h1t", [32, 128, 4096], F32, kind="ExternalInput").ap()
    h2_d = nc.dram_tensor("h2t", [32, 128, 4096], F32, kind="ExternalInput").ap()
    w3_d = nc.dram_tensor("w3t", [8, 128, 4096], F32, kind="ExternalInput").ap()
    bq_d = nc.dram_tensor("bqr", [1, D], F32, kind="ExternalInput").ap()
    bk_d = nc.dram_tensor("bkr", [1, D], F32, kind="ExternalInput").ap()
    bv_d = nc.dram_tensor("bvr", [1, D], F32, kind="ExternalInput").ap()
    bo_d = nc.dram_tensor("bor", [1, D], F32, kind="ExternalInput").ap()
    s1_d = nc.dram_tensor("s1v", [128, 32], F32, kind="ExternalInput").ap()
    s2_d = nc.dram_tensor("s2v", [128, 32], F32, kind="ExternalInput").ap()
    sh1_d = nc.dram_tensor("sh1v", [128, 32], F32, kind="ExternalInput").ap()
    sh2_d = nc.dram_tensor("sh2v", [128, 32], F32, kind="ExternalInput").ap()
    s3_d = nc.dram_tensor("s3v", [128, 8], F32, kind="ExternalInput").ap()
    mb_d = nc.dram_tensor("maskb", [128, NW * NJC], F32, kind="ExternalInput").ap()
    tri_d = nc.dram_tensor("tri01", [128, 2, 128], F32, kind="ExternalInput").ap()
    id_d = nc.dram_tensor("ident", [128, 128], F32, kind="ExternalInput").ap()
    out_d = nc.dram_tensor("out", [R, D], F32, kind="ExternalOutput").ap()
    x2s_d = nc.dram_tensor("x2s", [R, D], F32).ap()

    em = Emitter()
    ct = Ctr("ct", "t")
    cs = Ctr("cs", "s")
    cv = Ctr("cv", "v")
    cgm = Ctr("cgm", "g")
    cp = Ctr("cp")
    ctri = Ctr("ctri")
    cwb = Ctr("cwb")
    ch = [Ctr("ch0"), Ctr("ch1")]
    cl = [Ctr("cl0"), Ctr("cl1"), Ctr("cl2")]
    cst = [Ctr("cst0"), Ctr("cst1")]
    ctrs = [ct, cs, cv, cgm, cp, ctri, cwb] + ch + cl + cst

    alloc = []

    def sb(name, shape, dtype):
        cm = nc.sbuf_tensor(name, shape, dtype)
        h = cm.__enter__()
        alloc.append(cm)
        return h

    def psum(name, shape, dtype):
        cm = nc.psum_tensor(name, shape, dtype)
        h = cm.__enter__()
        alloc.append(cm)
        return h

    def free_to(n):
        while len(alloc) > n:
            alloc.pop().__exit__(None, None, None)

    def gdma(ctr, dst, src, waits=()):
        return em.op("g", lambda g, d=dst, s=src: g.dma_start(d, s), waits=waits,
                     inc=ctr, inck=16)

    def ydma(ctr, dst, src, waits=()):
        return em.op("y", lambda y, d=dst, s=src: y.dma_start(d, s), waits=waits,
                     inc=ctr, inck=16)

    ones_bf = sb("ones_bf", [128, 512], BF16)
    ones_f = sb("ones_f", [128, 128], F32)
    ident = sb("identt", [128, 128], BF16)
    kpad = sb("kpad", [128, 128], BF16)
    consts = sb("consts", [128, 2], F32)
    bq_s = sb("bq_s", [1, D], BF16)
    bk_s = sb("bk_s", [1, D], BF16)
    mb_s = sb("mb_s", [128, NW * NJC], F32)
    bo_s = sb("bo_s", [1, D], BF16)
    bv_s = sb("bv_s", [1, D], BF16)
    s1_s = sb("s1_s", [128, 32], F32)
    s2_s = sb("s2_s", [128, 32], F32)
    sh1_s = sb("sh1_s", [128, 32], F32)
    sh2_s = sb("sh2_s", [128, 32], F32)
    s3_s = sb("s3_s", [128, 8], F32)
    EPS = consts[:, 0:1]
    INVN = consts[:, 1:2]
    persist_mark = len(alloc)

    em.op("g", lambda g: g.memset(ones_bf[:], 1.0))
    em.op("g", lambda g: g.memset(ones_f[:], 1.0))
    em.op("g", lambda g: g.memset(kpad[:], -1.0))
    em.op("g", lambda g: g.memset(consts[:, 0:1], 1e-5))
    t_gm = em.op("g", lambda g: g.memset(consts[:, 1:2], 1.0 / 1024.0), inc=cgm)

    for dst, src in [(bq_s[:], bq_d), (bk_s[:], bk_d), (mb_s[:], mb_d),
                     (bo_s[:], bo_d), (bv_s[:], bv_d), (s1_s[:], s1_d),
                     (s2_s[:], s2_d), (sh1_s[:], sh1_d), (sh2_s[:], sh2_d),
                     (s3_s[:], s3_d), (ident[:], id_d)]:
        gdma(cp, dst, src)

    xln_own = sb("xln_own", [128, 8, R], BF16)
    k_fm = sb("k_fm", [128, 8, RH], BF16)
    v_tm = sb("v_tm", [128, RH // 128, H * 65], BF16)
    vpad = sb("vpad", [128, H * 65], BF16)
    wbuf = sb("wbuf", [128, 8, D], BF16)
    attn_mark = len(alloc)

    em.op("g", lambda g: g.memset(
        v_tm[:].rearrange("p c (h u) -> p c h u", u=65)[:, :, :, 64:65], 1.0))
    em.op("g", lambda g: g.memset(
        vpad[:].rearrange("p (h u) -> p h u", u=65)[:, :, 0:64], -512.0))
    t_vones = em.op("g", lambda g: g.memset(
        vpad[:].rearrange("p (h u) -> p h u", u=65)[:, :, 64:65], 512.0), inc=cgm)

    wbuf2 = sb("wbuf2", [128, 8, D], BF16)
    xbuf = sb("xbuf", [128, 3, D], F32)
    lntmp = sb("lntmp", [128, 24], F32)
    xln_tm = sb("xln_tm", [128, 3, D], BF16)
    xln_L = sb("xln_L", [128, 8, HALO], BF16)
    xln_R = sb("xln_R", [128, 8, HALO], BF16)
    kv_mark = len(alloc)

    ps_tr = psum("ps_tr", [128, 1024], BF16)
    ps_p1 = psum("ps_p1", [128, 1024], F32)
    ps_p2 = psum("ps_p2", [128, 1024], F32)
    ps_p3 = psum("ps_p3", [128, 1024], F32)
    ps_bc = psum("ps_bc", [128, 512], F32)
    psum_mark = len(alloc)

    sl_xbuf = [Slot() for _ in range(3)]
    sl_xlntm = [Slot() for _ in range(3)]
    sl_pstr = Slot()
    sl_ln = [Slot() for _ in range(3)]
    sl_p1 = Slot()
    sl_p2 = Slot()
    sl_p3 = Slot()
    sl_bc = Slot()
    sl_xld = [Slot() for _ in range(RH // 128)]
    sl_vch = [Slot() for _ in range(RH // 128)]
    tok_kblk = [None] * 6

    gdma(cp, wbuf[:], wk_d)
    gdma(cp, wbuf2[:], wv_d)
    tok_cp = (cp, cp.n)
    tok_wk = tok_cp
    tok_wv = tok_cp
    tok_small = tok_cp

    def xln_dest(tc):
        if tc < 4:
            return xln_L[:, :, 128 * tc:128 * (tc + 1)]
        if tc >= 20:
            return xln_R[:, :, 128 * (tc - 20):128 * (tc - 19)]
        return xln_own[:, :, 128 * (tc - 4):128 * (tc - 3)]

    def xln_block(nt):
        if nt == 0:
            return xln_L
        if nt == 5:
            return xln_R
        return None

    def xln_blk_ap(nt, ic):
        if nt == 0:
            return xln_L[:, ic, :]
        if nt == 5:
            return xln_R[:, ic, :]
        return xln_own[:, ic, 512 * (nt - 1):512 * nt]

    def emit_ln_chain(dram_rows, pbuf, ln_slots, tmps, slx, xlntm_t,
                      extra_dma_waits=()):
        (NSUM, NMEAN, SUMSQ, MSQ, VAR, LNV, RSTD) = tmps[pbuf]
        sl_xb, sl_xt, sl_ps, sl_l2 = ln_slots
        sl_l = sl_l2[pbuf]
        xb = slx[:, pbuf, :]
        tokx = ydma(cl[pbuf % len(cl)] if pbuf < 3 else cl[0], xb, dram_rows,
                    waits=sl_xb[pbuf].ww() + list(extra_dma_waits))
        sl_xb[pbuf].wrote(tokx)
        tokv = em.op("v", lambda v: v.reduce_sum(NSUM, xb, axis=AX, negate=True),
                     waits=[tokx, t_gm] + sl_l.ww(), inc=cv)
        em.op("v", lambda v: v.tensor_scalar(NMEAN, NSUM, INVN, None, op0=ALU.mult),
              drain=True)
        toks = em.op("s", lambda s: s.activation(xlntm_t[:, pbuf, :], xb, AF.Square,
                                                 accum_out=SUMSQ),
                     waits=[tokx, tokv] + sl_xt[pbuf].ww(), inc=cs)
        em.op("v", lambda v: v.tensor_mul(MSQ, NMEAN, NMEAN), waits=[toks],
              drain=True)
        em.op("v", lambda v: v.tensor_scalar(VAR, SUMSQ, INVN, None, op0=ALU.mult))
        tokv2 = em.op("v", lambda v: v.tensor_sub(VAR, VAR, MSQ), inc=cv, drain=True)
        em.op("s", lambda s: s.activation(LNV, VAR, AF.Ln, bias=EPS), waits=[tokv2])
        toks2 = em.op("s", lambda s: s.activation(RSTD, LNV, AF.Exp, scale=-0.5),
                      inc=cs, drain=True)
        tokv3 = em.op("v", lambda v: v.tensor_scalar(xlntm_t[:, pbuf, :], xb, NMEAN,
                                                     RSTD, op0=ALU.add, op1=ALU.mult),
                      waits=[toks2, toks], inc=cv, drain=True)
        sl_l.wrote(tokv3)
        sl_xb[pbuf].read(tokv3)
        sl_xt[pbuf].wrote(tokv3)
        return tokv3

    def emit_ln_tr(dest_ap, dest_slot, pbuf, ln_slots, psx, xlntm_t, tokv3):
        sl_xb, sl_xt, sl_ps, sl_l2 = ln_slots
        tokt = None
        for c in range(8):
            tokt = em.op("t", lambda t, c=c: t.transpose(
                psx[:, 128 * c:128 * (c + 1)],
                xlntm_t[:, pbuf, 128 * c:128 * (c + 1)], ident[:]),
                waits=(sl_ps.ww() + [tokv3, tok_small] if c == 0 else []),
                inc=(ct if c == 7 else None))
        sl_xt[pbuf].read(tokt)
        sl_ps.wrote(tokt)
        tokev = em.op("v", lambda v: v.tensor_copy(
            dest_ap,
            psx[:].rearrange("p (c f) -> p c f", f=128)),
            waits=[tokt] + dest_slot.ww(), inc=cv)
        sl_ps.read(tokev)
        dest_slot.wrote(tokev)
        return tokev

    def emit_ln(dram_rows, dest_ap, dest_slot, pbuf, ln_slots, tmps, psx, slx,
                xlntm_t, extra_dma_waits=()):
        tokv3 = emit_ln_chain(dram_rows, pbuf, ln_slots, tmps, slx, xlntm_t,
                              extra_dma_waits)
        return emit_ln_tr(dest_ap, dest_slot, pbuf, ln_slots, psx, xlntm_t, tokv3)

    ln_slots = (sl_xbuf, sl_xlntm, sl_pstr, sl_ln)
    tmps = [tuple(lntmp[:, 8 * par + i:8 * par + i + 1] for i in range(7))
            for par in range(3)]

    ln1_pend = []
    for tc in range(RH // 128 + 2):
        if tc < RH // 128:
            tokv3 = emit_ln_chain(xh_d[128 * tc:128 * (tc + 1), :], tc % 3,
                                  ln_slots, tmps, xbuf, xln_tm)
            ln1_pend.append((tc, tokv3))
        if len(ln1_pend) < (2 if tc < RH // 128 else 1):
            continue
        tc, tokv3 = ln1_pend.pop(0)
        emit_ln_tr(xln_dest(tc), sl_xld[tc], tc % 3, ln_slots, ps_tr, xln_tm,
                   tokv3)
        xsrc = xln_dest(tc)
        tokm = None
        for ic in range(8):
            for on in range(2):
                tokm = em.op("t", lambda t, on=on, ic=ic, xsrc=xsrc: t.matmul(
                    ps_p1[:, 512 * on:512 * (on + 1)],
                    xsrc[:, ic, :],
                    wbuf2[:, ic, 512 * on:512 * (on + 1)],
                    start=(ic == 0), stop=False),
                    waits=(sl_p1.ww() + [sl_xld[tc].w, tok_wv]
                           if (on == 0 and ic == 0) else []))
        for on in range(2):
            tokm = em.op("t", lambda t, on=on: t.matmul(
                ps_p1[:, 512 * on:512 * (on + 1)], ones_bf[0:1, 0:128],
                bv_s[0:1, 512 * on:512 * (on + 1)], start=False, stop=True),
                inc=(ct if on == 1 else None))
        sl_p1.wrote(tokm)
        sl_xld[tc].read(tokm)
        toke = em.op("s", lambda s, tc=tc: s.activation(
            v_tm[:, tc, :].rearrange("p (h u) -> p h u", u=65)[:, :, 0:64],
            ps_p1[:].rearrange("p (h u) -> p h u", u=64), AF.Copy),
            waits=[tokm] + sl_vch[tc].ww(), inc=cs)
        sl_p1.read(toke)
        sl_vch[tc].wrote(toke)
        if tc % 4 == 3:
            nt = tc // 4
            blk_deps = [sl_xld[j].w for j in range(4 * nt, 4 * nt + 4)]
            toke_k = None
            for oc in range(8):
                pk = ps_p2 if oc % 2 == 0 else ps_p3
                slk = sl_p2 if oc % 2 == 0 else sl_p3
                half = (oc // 2) % 2
                tokmk = None
                for ic in range(8):
                    em.op("t", lambda t, oc=oc, ic=ic, half=half, nt=nt,
                          pk=pk: t.matmul(
                        pk[:, 512 * half:512 * (half + 1)],
                        wbuf[:, ic, 128 * oc:128 * (oc + 1)],
                        xln_blk_ap(nt, ic),
                        start=(ic == 0), stop=False),
                        waits=(slk.ww() + blk_deps + [tok_wk] if ic == 0 else []))
                tokmk = em.op("t", lambda t, oc=oc, half=half, pk=pk: t.matmul(
                    pk[:, 512 * half:512 * (half + 1)],
                    bk_s[0:1, 128 * oc:128 * (oc + 1)], ones_bf[0:1, 0:512],
                    start=False, stop=True), inc=ct)
                slk.wrote(tokmk)
                toke_k = em.op("s", lambda s, oc=oc, nt=nt, half=half, pk=pk:
                               s.activation(
                    k_fm[:, oc, 512 * nt:512 * (nt + 1)],
                    pk[:, 512 * half:512 * (half + 1)],
                    AF.Copy),
                    waits=[tokmk], inc=cs)
                slk.read(toke_k)
            tok_kblk[nt] = toke_k

    tok_kv_done = (ct, ct.n)

    free_to(attn_mark)
    tri_s = sb("tri_s", [128, 2, 128], BF16)
    q_fm = sb("q_fm", [128, 8, 512], BF16)
    probsT = sb("probsT", [128, 3, 1024], BF16)
    attn_un = sb("attn_un", [128, 512], F32)
    attn_fm = sb("attn_fm", [128, 2, 8, 512], BF16)
    rsum_sb = sb("rsum_sb", [128, 1024], F32)
    x2buf = sb("x2buf", [128, 1, D], F32)
    xbuf = sb("xbuf_w", [128, 2, D], F32)[:].tensor if False else sb("xbuf_w", [128, 1, D], F32)
    ps_scp = [psum("ps_scA", [128, 1024], F32), psum("ps_scB", [128, 1024], F32),
              psum("ps_scC", [128, 1024], F32)]
    ps_pv2 = psum("ps_pv2", [128, 1024], F32)

    bar = [(ct, ct.n), (cs, cs.n), (cv, cv.n)]
    sl_xbuf = [Slot(bar), Slot(bar)]
    sl_q = Slot(bar)
    sl_probs = [Slot(bar), Slot(bar), Slot(bar)]
    sl_attnun = [Slot(bar), Slot(bar)]
    sl_attnfm = [Slot(bar), Slot(bar)]
    sl_rsum = [Slot(bar), Slot(bar)]
    sl_x2buf = [Slot(bar), Slot(bar)]
    sl_wbuf = Slot(bar)
    sl_scp = [Slot(bar), Slot(bar), Slot(bar)]
    sl_pv = [Slot(bar), Slot(bar)]

    tok_tri = gdma(ctri, tri_s[:], tri_d,
                   waits=[tok_kv_done, (cs, cs.n), (cv, cv.n)])
    tok_w = gdma(cwb, wbuf[:], wq_d,
                 waits=[tok_kv_done, (cs, cs.n), (cv, cv.n)])
    sl_wbuf.wrote(tok_w)

    pend_tails = []
    attn_done_tok = [None, None]

    def emit_oproj_group(wo_, tc):
        wpar = wo_ % 2
        pb2 = tc % 2
        rows = xh_d[HALO + 512 * wo_ + 128 * tc: HALO + 512 * wo_ + 128 * (tc + 1), :]
        tokx = ydma(cl[pb2], xbuf[:, 0, :], rows, waits=sl_xbuf[0].ww())
        sl_xbuf[0].wrote(tokx)
        pko = ps_scp[2]
        pslo = sl_scp[2]
        tokmo = None
        for ic in range(8):
            for on in range(2):
                tokmo = em.op("t", lambda t, on=on, ic=ic, tc=tc, pko=pko,
                              wpar=wpar: t.matmul(
                    pko[:, 512 * on:512 * (on + 1)],
                    attn_fm[:, wpar, ic, 128 * tc:128 * (tc + 1)],
                    wbuf[:, ic, 512 * on:512 * (on + 1)],
                    start=(ic == 0), stop=False),
                    waits=(pslo.ww() + [attn_done_tok[wpar], sl_wbuf.w]
                           if (on == 0 and ic == 0) else []))
        for on in range(2):
            tokmo = em.op("t", lambda t, on=on, pko=pko: t.matmul(
                pko[:, 512 * on:512 * (on + 1)], ones_bf[0:1, 0:128],
                bo_s[0:1, 512 * on:512 * (on + 1)], start=False, stop=True),
                inc=(ct if on == 1 else None))
        pslo.wrote(tokmo)
        sl_wbuf.read(tokmo)
        if tc == 3:
            sl_attnfm[wpar].read(tokmo)
        toka = em.op("v", lambda v, pko=pko: v.tensor_add(
            x2buf[:, 0, :], pko[:, :], xbuf[:, 0, :]),
            waits=[tokmo, tokx] + sl_x2buf[0].ww(), inc=cv)
        pslo.read(toka)
        sl_xbuf[0].read(toka)
        sl_x2buf[0].wrote(toka)
        drows = (x2s_d if mode == "full" else out_d)[
            512 * wo_ + 128 * tc: 512 * wo_ + 128 * (tc + 1), :]
        tokst = ydma(cst[pb2], drows, x2buf[:, 0, :], waits=[toka])
        sl_x2buf[0].read(tokst)

    for w in range(NW):
        toke_q = None
        for oc in range(8):
            half = oc % 2
            pq = ps_scp[half]
            slq = sl_scp[half]
            for ic in range(8):
                em.op("t", lambda t, oc=oc, ic=ic, w=w, pq=pq: t.matmul(
                    pq[:, 0:512],
                    wbuf[:, ic, 128 * oc:128 * (oc + 1)],
                    xln_own[:, ic, 512 * w:512 * (w + 1)],
                    start=(ic == 0), stop=False),
                    waits=(slq.ww() + [sl_wbuf.w] if ic == 0 else []))
            tokmq = em.op("t", lambda t, oc=oc, pq=pq: t.matmul(
                pq[:, 0:512],
                bq_s[0:1, 128 * oc:128 * (oc + 1)], ones_bf[0:1, 0:512],
                start=False, stop=True), inc=ct)
            slq.wrote(tokmq)
            toke_q = em.op("s", lambda s, oc=oc, pq=pq: s.activation(
                q_fm[:, oc, :], pq[:, 0:512],
                AF.Copy),
                waits=[tokmq] + (sl_q.ww() if oc == 0 else []), inc=cs)
            slq.read(toke_q)
            sl_wbuf.read(tokmq)
        sl_q.wrote(toke_q)

        tok_w = gdma(cwb, wbuf[:], wo_d, waits=sl_wbuf.ww())
        sl_wbuf.wrote(tok_w)

        CORDER = [4, 5, 6, 7, 12, 0, 1, 2, 3, 8, 9, 10, 11]
        for hc in range(H // 2):
            exp_toks = [None] * 13
            sc_toks = [None] * 13

            def emit_scores(ci, pb, hc=hc, w=w, sc_toks=sc_toks):
                jc = CORDER[ci]
                pps = ps_scp[pb]
                off, wid = _band(jc)
                if jc < 12:
                    kof = 512 * w + 128 * jc
                    kdep = tok_kblk[kof // 512]
                else:
                    kdep = t_gm
                tokm = None
                for hp in range(2):
                    hoff = 64 * hp
                    if jc < 12:
                        klhs = k_fm[hoff:hoff + 64, hc, kof:kof + 128]
                    else:
                        klhs = kpad[hoff:hoff + 64, :]
                    tokm = em.op("t", lambda t, klhs=klhs, hp=hp, pps=pps,
                                 hoff=hoff, hc=hc, off=off, wid=wid: t.matmul(
                        pps[:, 512 * hp + off:512 * hp + off + wid],
                        klhs, q_fm[hoff:hoff + 64, hc, off:off + wid],
                        start=True, stop=True),
                        waits=(sl_scp[pb].ww() + [sl_q.w, kdep]) if hp == 0 else [],
                        inc=(ct if hp == 1 else None))
                sl_scp[pb].wrote(tokm)
                sl_q.read(tokm)
                sc_toks[ci] = tokm

            def emit_exp(ci, pb, w=w, sc_toks=sc_toks, exp_toks=exp_toks,
                         extra=()):
                jc = CORDER[ci]
                pps = ps_scp[pb]
                off, wid = _band(jc)
                ws_ = [sc_toks[ci], tok_small] + list(extra) + sl_probs[pb].ww()
                toke = None
                if wid == 512:
                    toke = em.op("s", lambda s, pb=pb, pps=pps, jc=jc, w=w:
                                 s.activation(
                        probsT[:, pb, 0:1024], pps[:, 0:1024],
                        AF.Exp, bias=mb_s[:, w * NJC + jc:w * NJC + jc + 1]),
                        waits=ws_, inc=cs)
                else:
                    for hp in range(2):
                        lo = 512 * hp + off
                        toke = em.op("s", lambda s, pb=pb, pps=pps, jc=jc, w=w,
                                     lo=lo, wid=wid: s.activation(
                            probsT[:, pb, lo:lo + wid], pps[:, lo:lo + wid],
                            AF.Exp, bias=mb_s[:, w * NJC + jc:w * NJC + jc + 1]),
                            waits=ws_, inc=cs)
                        ws_ = []
                sl_scp[pb].read(toke)
                sl_probs[pb].wrote(toke)
                ptok = toke
                if jc < 4:
                    tlo, tt = 128 * jc, 0
                elif 8 <= jc < 12:
                    tlo, tt = 128 * (jc - 8), 1
                else:
                    tlo = None
                if tlo is not None:
                    for hp in range(2):
                        lo = 512 * hp + tlo
                        ptok = em.op("v", lambda v, pb=pb, lo=lo, tt=tt:
                                     v.tensor_mul(
                            probsT[:, pb, lo:lo + 128], probsT[:, pb, lo:lo + 128],
                            tri_s[:, tt, :]),
                            waits=[toke, tok_tri], inc=cv)
                    sl_probs[pb].wrote(ptok)
                exp_toks[ci] = ptok

            def emit_pv(ci, pb, first_pv, last_pv, hc=hc, w=w,
                        exp_toks=exp_toks, inc_always=False):
                jc = CORDER[ci]
                off, wid = _band(jc)
                tokpv = None
                for hp in range(2):
                    h = 2 * hc + hp
                    if jc < 12:
                        vl = v_tm[:, 4 * w + jc, :].rearrange(
                            "p (h u) -> p h u", u=65)[:, h, :]
                        vdep = sl_vch[4 * w + jc].w
                    else:
                        vl = vpad[:].rearrange("p (h u) -> p h u", u=65)[:, h, :]
                        vdep = t_vones
                    pvlo = 512 * hp
                    tokpv = em.op("t", lambda t, vl=vl, pb=pb, hp=hp, pvlo=pvlo,
                                  off=off, wid=wid, st_=first_pv, sp_=last_pv:
                                  t.matmul(
                        ps_pv2[0:65, pvlo + off:pvlo + off + wid],
                        vl, probsT[:, pb, 512 * hp + off:512 * hp + off + wid],
                        start=st_, stop=sp_),
                        waits=[exp_toks[ci], vdep]
                        + (sl_pv[hp].ww() if first_pv else []),
                        inc=(ct if ((last_pv or inc_always) and hp == 1)
                             else None))
                return tokpv

            for i0 in range(3):
                emit_scores(i0, i0 % 3)
                emit_exp(i0, i0 % 3)
            for i0 in range(3, 6):
                emit_scores(i0, i0 % 3)
            while pend_tails:
                pend_tails.pop(0)()
            for i in range(13):
                tokpv = emit_pv(i, i % 3, first_pv=(i == 0), last_pv=(i == 12),
                                inc_always=(i < 3))
                if i < 3:
                    emit_exp(i + 3, (i + 3) % 3, extra=[tokpv])
                elif i + 3 < 13:
                    emit_scores(i + 3, (i + 3) % 3)
                    emit_exp(i + 3, (i + 3) % 3)
            sl_pv[0].wrote(tokpv)
            sl_pv[1].wrote(tokpv)
            for sp in sl_probs:
                sp.read(tokpv)
            for hp in range(2):
                pvlo = 512 * hp
                hoff = 64 * hp
                tokr = em.op("v", lambda v, pvlo=pvlo: v.reciprocal(
                    rsum_sb[64:65, pvlo:pvlo + 512],
                    ps_pv2[64:65, pvlo:pvlo + 512]),
                    waits=[tokpv] + sl_rsum[hp].ww(), inc=cv)
                sl_rsum[hp].wrote(tokr)
                toke2 = em.op("s", lambda s, pvlo=pvlo, hoff=hoff: s.activation(
                    attn_un[hoff:hoff + 64, :], ps_pv2[0:64, pvlo:pvlo + 512],
                    AF.Copy),
                    waits=[tokpv] + sl_attnun[hp].ww(), inc=cs)
                sl_attnun[hp].wrote(toke2)
                sl_pv[hp].read(tokr)
                sl_pv[hp].read(toke2)

                def mk_tail(hp=hp, tokr=tokr, toke2=toke2, hc=hc, wpar=w % 2):
                    def run():
                        hoff = 64 * hp
                        pvlo = 512 * hp
                        tokb = em.op("t", lambda t: t.matmul(
                            ps_pv2[:, pvlo:pvlo + 512], ones_f[64:65, :],
                            rsum_sb[64:65, pvlo:pvlo + 512],
                            start=True, stop=True),
                            waits=[tokr, toke2, t_gm] + sl_pv[hp].ww(), inc=ct)
                        sl_rsum[hp].read(tokb)
                        sl_pv[hp].wrote(tokb)
                        tokn = em.op("v", lambda v: v.tensor_mul(
                            attn_fm[hoff:hoff + 64, wpar, hc, :],
                            attn_un[hoff:hoff + 64, :],
                            ps_pv2[hoff:hoff + 64, pvlo:pvlo + 512]),
                            waits=[tokb, toke2]
                            + (sl_attnfm[wpar].ww() if hc == 0 else []), inc=cv)
                        sl_attnun[hp].read(tokn)
                        sl_pv[hp].read(tokn)
                        if hc == H // 2 - 1 and hp == 1:
                            sl_attnfm[wpar].wrote(tokn)
                            attn_done_tok[wpar] = tokn
                    return run
                pend_tails.append(mk_tail())
            if w >= 1 and hc in (2, 4, 6, 7):
                emit_oproj_group(w - 1, {2: 0, 4: 1, 6: 2, 7: 3}[hc])

        if w < NW - 1:
            tok_w = gdma(cwb, wbuf[:], wq_d, waits=sl_wbuf.ww())
            sl_wbuf.wrote(tok_w)

    while pend_tails:
        pend_tails.pop(0)()
    for tc in range(4):
        emit_oproj_group(NW - 1, tc)

    if mode == "attn":
        _finalize(nc, em, ctrs, [(cst[0], cst[0].n), (cst[1], cst[1].n)])
        return nc

    free_to(persist_mark)
    bar = [(ct, ct.n), (cs, cs.n), (cv, cv.n),
           (cst[0], cst[0].n), (cst[1], cst[1].n),
           (cl[0], cl[0].n), (cl[1], cl[1].n), (cwb, cwb.n), (ctri, ctri.n)]

    ps_tr2 = psum("ps_tr2", [128, 1024], BF16)
    ps_fa = psum("ps_fa", [128, 1024], F32)
    ps_fb = psum("ps_fb", [128, 1024], F32)

    x2in = sb("x2in", [128, 2, D], F32)
    lntmp2 = sb("lntmp2", [128, 16], F32)
    xln2_tm = sb("xln2_tm", [128, 2, D], BF16)
    xln2_fm = sb("xln2_fm", [128, 8, TP], BF16)
    y_fm = sb("y_fm", [128, 8, TP], BF16)
    a_fm = sb("a_fm", [128, 32, TP], BF16)
    b_fm = sb("b_fm", [128, 32, TP], BF16)
    hcol = sb("hcol", [128, 2, 4096], BF16)
    t1_sb = sb("t1_sb", [128, TP], BF16)
    osb = sb("osb", [128, 1, D], F32)

    tmps2 = [tuple(lntmp2[:, 8 * par + i:8 * par + i + 1] for i in range(7))
             for par in range(2)]

    sl_x2in = [Slot(bar), Slot(bar)]
    sl_x2tm = [Slot(bar), Slot(bar)]
    sl_ln2 = [Slot(bar), Slot(bar)]
    sl_ps2 = Slot(bar)
    sl_xln2 = Slot(bar)
    sl_afm = Slot(bar)
    sl_bfm = Slot(bar)
    sl_hcol = [Slot(bar), Slot(bar)]
    sl_t1 = Slot(bar)
    sl_fa = Slot(bar)
    sl_fb = Slot(bar)
    sl_osb = [Slot(bar), Slot(bar)]
    sl_yfm = Slot(bar)
    ln2_slots = (sl_x2in, sl_x2tm, sl_ps2, sl_ln2)

    x2_stores = [(cst[0], cst[0].n), (cst[1], cst[1].n)]

    def emit_ln2(p):
        tokev = None
        pend = []
        for tc in range(TP // 128 + 2):
            if tc < TP // 128:
                row0 = TP * p + 128 * tc
                tokv3 = emit_ln_chain(x2s_d[row0:row0 + 128, :], tc % 2,
                                      ln2_slots, tmps2, x2in, xln2_tm,
                                      extra_dma_waits=x2_stores)
                pend.append((tc, tokv3))
            if len(pend) < (2 if tc < TP // 128 else 1):
                continue
            tcp, tokv3 = pend.pop(0)
            tokev = emit_ln_tr(xln2_fm[:, :, 128 * tcp:128 * (tcp + 1)],
                               sl_xln2 if tcp == TP // 128 - 1 else Slot(),
                               tcp % 2, ln2_slots, ps_tr2, xln2_tm, tokv3)
        sl_xln2.wrote(tokev)

    emit_ln2(0)
    for p in range(NPASS):
        tokav = None
        for oc in range(32):
            hb = oc % 2
            tokw1 = gdma(ch[hb], hcol[:, hb, 0:1024], w1_d[oc],
                         waits=sl_hcol[hb].ww())
            tokw2 = gdma(ch[hb], hcol[:, hb, 1024:2048], w2_d[oc])
            sl_hcol[hb].wrote(tokw2)
            tokm1 = None
            for nt in range(TP // 512):
                for ic in range(8):
                    tokm1 = em.op("t", lambda t, ic=ic, nt=nt, hb=hb: t.matmul(
                        ps_fa[:, 512 * nt:512 * (nt + 1)],
                        hcol[:, hb, 128 * ic:128 * (ic + 1)],
                        xln2_fm[:, ic, 512 * nt:512 * (nt + 1)],
                        start=(ic == 0), stop=(ic == 7)),
                        waits=(sl_fa.ww() + [tokw2, sl_xln2.w]
                               if (nt == 0 and ic == 0) else []),
                        inc=(ct if (nt == TP // 512 - 1 and ic == 7) else None))
            sl_fa.wrote(tokm1)
            tokm2 = None
            for nt in range(TP // 512):
                for ic in range(8):
                    tokm2 = em.op("t", lambda t, ic=ic, nt=nt, hb=hb: t.matmul(
                        ps_fb[:, 512 * nt:512 * (nt + 1)],
                        hcol[:, hb, 1024 + 128 * ic:1024 + 128 * (ic + 1)],
                        xln2_fm[:, ic, 512 * nt:512 * (nt + 1)],
                        start=(ic == 0), stop=(ic == 7)),
                        waits=(sl_fb.ww() if (nt == 0 and ic == 0) else []),
                        inc=(ct if (nt == TP // 512 - 1 and ic == 7) else None))
            sl_fb.wrote(tokm2)
            sl_hcol[hb].read(tokm2)
            sl_xln2.read(tokm2)
            toksl = em.op("s", lambda s, oc=oc: s.activation(
                t1_sb[:], ps_fa[:, :], AF.Silu, scale=s1_s[:, oc:oc + 1]),
                waits=[tokm1] + sl_t1.ww(), inc=cs)
            sl_fa.read(toksl)
            sl_t1.wrote(toksl)
            tokav = em.op("v", lambda v, oc=oc: v.scalar_tensor_tensor(
                a_fm[:, oc, :], ps_fb[:, :], s2_s[:, oc:oc + 1], t1_sb[:],
                op0=ALU.mult, op1=ALU.mult),
                waits=[tokm2, toksl] + (sl_afm.ww() if oc == 0 else []), inc=cv)
            sl_fb.read(tokav)
            sl_t1.read(tokav)
        sl_afm.wrote(tokav)

        def big_stage(src_fm, src_slot, dst_fm, dst_slot, w_dram, sc_ap, nob, act,
                      nic, filler=None):
            tokl = None
            for oc in range(nob):
                if filler is not None:
                    filler(oc)
                hb = oc % 2
                pso = ps_fa if oc % 2 == 0 else ps_fb
                slo = sl_fa if oc % 2 == 0 else sl_fb
                tokw = gdma(ch[hb], hcol[:, hb, 0:128 * nic],
                            w_dram[oc][:, 0:128 * nic], waits=sl_hcol[hb].ww())
                sl_hcol[hb].wrote(tokw)
                tokm = None
                for nt in range(TP // 512):
                    for ic in range(nic):
                        tokm = em.op("t", lambda t, ic=ic, nt=nt, hb=hb, pso=pso:
                                     t.matmul(
                            pso[:, 512 * nt:512 * (nt + 1)],
                            hcol[:, hb, 128 * ic:128 * (ic + 1)],
                            src_fm[:, ic, 512 * nt:512 * (nt + 1)],
                            start=(ic == 0), stop=(ic == nic - 1)),
                            waits=(slo.ww() + [tokw, src_slot.w]
                                   if (nt == 0 and ic == 0) else []),
                            inc=(ct if (nt == TP // 512 - 1 and ic == nic - 1)
                                 else None))
                slo.wrote(tokm)
                sl_hcol[hb].read(tokm)
                src_slot.read(tokm)
                tokl = em.op("s", lambda s, oc=oc, pso=pso: s.activation(
                    dst_fm[:, oc, :], pso[:, :], act, scale=sc_ap[:, oc:oc + 1]),
                    waits=[tokm] + (dst_slot.ww() if oc == 0 else []), inc=cs)
                slo.read(tokl)
            dst_slot.wrote(tokl)
            return tokl

        if p + 1 < NPASS:
            ln2_state = [0, None]

            def ln2_filler(oc, p=p):
                tc_n, pend = ln2_state
                if oc % 2 == 1 and pend is not None:
                    tc, tokv3 = pend
                    tokev = emit_ln_tr(xln2_fm[:, :, 128 * tc:128 * (tc + 1)],
                                       sl_xln2 if tc == TP // 128 - 1 else Slot(),
                                       tc % 2, ln2_slots, ps_tr2, xln2_tm, tokv3)
                    if tc == TP // 128 - 1:
                        sl_xln2.wrote(tokev)
                    ln2_state[1] = None
                elif oc % 2 == 0 and tc_n < TP // 128 and pend is None:
                    row0 = TP * (p + 1) + 128 * tc_n
                    tokv3 = emit_ln_chain(x2s_d[row0:row0 + 128, :], tc_n % 2,
                                          ln2_slots, tmps2, x2in, xln2_tm,
                                          extra_dma_waits=x2_stores)
                    ln2_state[0] = tc_n + 1
                    ln2_state[1] = (tc_n, tokv3)
        else:
            ln2_filler = None
        big_stage(a_fm, sl_afm, b_fm, sl_bfm, h1_d, sh1_s, 32, AF.Gelu, 32,
                  filler=ln2_filler)
        big_stage(b_fm, sl_bfm, a_fm, sl_afm, h2_d, sh2_s, 32, AF.Gelu, 32)
        toky = big_stage(a_fm, sl_afm, y_fm, sl_yfm, w3_d, s3_s, 8, AF.Copy, 32)

        for tc in range(TP // 128):
            row0 = TP * p + 128 * tc
            pb2 = tc % 2
            tokx = ydma(cl[pb2], x2in[:, pb2, :], x2s_d[row0:row0 + 128, :],
                        waits=sl_x2in[pb2].ww() + x2_stores)
            sl_x2in[pb2].wrote(tokx)
            tokt = None
            for c in range(8):
                tokt = em.op("t", lambda t, c=c, tc=tc: t.transpose(
                    ps_tr2[:, 128 * c:128 * (c + 1)],
                    y_fm[:, c, 128 * tc:128 * (tc + 1)], ident[:]),
                    waits=(sl_ps2.ww() + [toky] if c == 0 else []),
                    inc=(ct if c == 7 else None))
            sl_ps2.wrote(tokt)
            sl_yfm.read(tokt)
            toko = em.op("v", lambda v, pb2=pb2: v.tensor_add(
                osb[:, 0, :], ps_tr2[:, :], x2in[:, pb2, :]),
                waits=[tokt, tokx] + sl_osb[0].ww(), inc=cv)
            sl_ps2.read(toko)
            sl_x2in[pb2].read(toko)
            sl_osb[0].wrote(toko)
            tokst = ydma(cst[pb2], out_d[row0:row0 + 128, :], osb[:, 0, :],
                         waits=[toko])
            sl_osb[0].read(tokst)

    _finalize(nc, em, ctrs, [(cst[0], cst[0].n), (cst[1], cst[1].n)])
    return nc


def _finalize(nc, em, ctrs, final_tok):
    import contextlib
    with contextlib.ExitStack() as st:
        sems = {c.name: st.enter_context(nc.semaphore(c.name)) for c in ctrs}
        with nc.Block() as block:

            @block.tensor
            def _(eng):
                em.replay("t", eng, sems)

            @block.scalar
            def _(eng):
                em.replay("s", eng, sems)

            @block.vector
            def _(eng):
                em.replay("v", eng, sems)

            @block.gpsimd
            def _(eng):
                em.replay("g", eng, sems)

            @block.sync
            def _(eng):
                em.replay("y", eng, sems)
                for c, v in final_tok:
                    eng.wait_ge(sems[c.name], v)


def _prep_shared(L0_attn_wq, L0_attn_bq, L0_attn_wk, L0_attn_bk, L0_attn_wv,
                 L0_attn_bv, L0_attn_wo, L0_attn_bo, L0_ffn_w1_w, L0_ffn_w1_s,
                 L0_ffn_w2_w, L0_ffn_w2_s, L0_ffn_h1_w, L0_ffn_h1_s, L0_ffn_h2_w,
                 L0_ffn_h2_s, L0_ffn_w3_w, L0_ffn_w3_s):
    f = np.float32
    c = np.ascontiguousarray

    def wt_qkvo(w):
        return c(np.asarray(w, f).T.reshape(8, 128, D).transpose(1, 0, 2))

    def tile_in(wT, nic, noc):
        t = np.asarray(wT, f).reshape(nic, 128, noc, 128).transpose(2, 1, 0, 3)
        return c(t.reshape(noc, 128, nic * 128))

    d = {
        "wqT": wt_qkvo(L0_attn_wq), "wkT": wt_qkvo(L0_attn_wk),
        "wvT": wt_qkvo(L0_attn_wv), "woT": wt_qkvo(L0_attn_wo),
        "w1t": tile_in(np.asarray(L0_ffn_w1_w, f).T, 8, 32),
        "w2t": tile_in(np.asarray(L0_ffn_w2_w, f).T, 8, 32),
        "h1t": tile_in(np.asarray(L0_ffn_h1_w, f).T, 32, 32),
        "h2t": tile_in(np.asarray(L0_ffn_h2_w, f).T, 32, 32),
        "w3t": tile_in(np.asarray(L0_ffn_w3_w, f).T, 32, 8),
        "bqr": np.asarray(L0_attn_bq, f).reshape(1, D).copy(),
        "bkr": np.asarray(L0_attn_bk, f).reshape(1, D).copy(),
        "bvr": np.asarray(L0_attn_bv, f).reshape(1, D).copy(),
        "bor": np.asarray(L0_attn_bo, f).reshape(1, D).copy(),
        "s1v": c(np.asarray(L0_ffn_w1_s, f).reshape(32, 128).T),
        "s2v": c(np.asarray(L0_ffn_w2_s, f).reshape(32, 128).T),
        "sh1v": c(np.asarray(L0_ffn_h1_s, f).reshape(32, 128).T),
        "sh2v": c(np.asarray(L0_ffn_h2_s, f).reshape(32, 128).T),
        "s3v": c(np.asarray(L0_ffn_w3_s, f).reshape(8, 128).T),
        "ident": np.eye(128, dtype=f),
    }
    pp = np.arange(128)[:, None]
    ii = np.arange(128)[None, :]
    tri = np.zeros((128, 2, 128), f)
    tri[:, 0, :] = (ii <= pp)
    tri[:, 1, :] = (ii >= pp)
    d["tri01"] = tri
    return d


def _prep_core(x, core):
    f = np.float32
    b = core // 4
    T0 = (core % 4) * R
    xh = np.zeros((RH, D), f)
    lo = max(0, T0 - HALO)
    hi = min(S_SEQ, T0 + R + HALO)
    xh[lo - (T0 - HALO): hi - (T0 - HALO), :] = np.asarray(x[b, lo:hi, :], f)
    mb = np.empty((128, NW * NJC), f)
    for w in range(NW):
        for jc in range(12):
            g = T0 - HALO + 512 * w + 128 * jc + np.arange(128)
            real = (g >= 0) & (g < S_SEQ)
            mb[:, w * NJC + jc] = np.where(real, SHIFT, SHIFT + MASKV)
        edge = (T0 + 512 * w == 0) or (T0 + 512 * w + 512 == S_SEQ)
        col = np.full(128, SHIFT + MASKV, f)
        if edge:
            col[0] = SHIFT
        mb[:, w * NJC + 12] = col
    return {"xh": xh, "maskb": mb}


def kernel(**inputs):
    mode = inputs.pop("_mode", "full")
    x = np.asarray(inputs.pop("L0_attn_x"), np.float32)
    shared = _prep_shared(**inputs)
    in_maps = []
    for core in range(N_CORES):
        m = dict(shared)
        m.update(_prep_core(x, core))
        in_maps.append(m)
    nc = build(mode)
    res = run_bass_kernel_spmd(nc, in_maps, core_ids=list(range(N_CORES)))
    parts = [res.results[core]["out"] for core in range(N_CORES)]
    out = np.empty((B_BATCH, S_SEQ, D), np.float32)
    for core in range(N_CORES):
        b = core // 4
        T0 = (core % 4) * R
        out[b, T0:T0 + R, :] = parts[core]
    return out


# version 23
# speedup vs baseline: 1.0460x; 1.0030x over previous
import numpy as np
import concourse.bass as bass
import concourse.mybir as mybir
from concourse.bass_utils import run_bass_kernel_spmd

F32 = mybir.dt.float32
BF16 = mybir.dt.bfloat16
AF = mybir.ActivationFunctionType
ALU = mybir.AluOpType
AX = mybir.AxisListType.X

D = 1024
H = 16
R = 2048
HALO = 512
RH = R + 2 * HALO
NW = 4
NJC = 13
DH = 4096
TP = 1024
NPASS = R // TP
SHIFT = -24.0
MASKV = -30000.0

N_CORES = 8
S_SEQ = 8192
B_BATCH = 2


def _band(jc):
    if jc < 4:
        return 0, 128 * (jc + 1)
    if jc < 8 or jc == 12:
        return 0, 512
    off = 128 * (jc - 8)
    return off, 512 - off


class Ctr:
    __slots__ = ("name", "eng", "n")

    def __init__(self, name, eng=None):
        self.name, self.eng, self.n = name, eng, 0


class Emitter:
    ENGS = ("t", "s", "v", "g", "y")

    def __init__(self):
        self.ops = {e: [] for e in self.ENGS}
        self.last_wait = {e: {} for e in self.ENGS}

    def op(self, eng, fn, waits=(), inc=None, inck=1, drain=False):
        wl = {}
        for tok in waits:
            if tok is None:
                continue
            c, v = tok
            if c.eng == eng or v <= 0:
                continue
            if wl.get(c, 0) < v:
                wl[c] = v
        resolved = []
        lw = self.last_wait[eng]
        for c, v in wl.items():
            if lw.get(c, 0) >= v:
                continue
            lw[c] = v
            resolved.append((c, v))
        tok = None
        if inc is not None:
            inc.n += inck
            tok = (inc, inc.n)
        self.ops[eng].append((resolved, fn, inc, inck, drain))
        return tok

    def replay(self, key, eng, sems):
        for waits, fn, inc, inck, drain in self.ops[key]:
            if drain:
                eng.drain()
            for c, v in waits:
                eng.wait_ge(sems[c.name], v)
            inst = fn(eng)
            if inc is not None:
                inst.then_inc(sems[inc.name], inck)


class Slot:
    __slots__ = ("w", "r")

    def __init__(self, seed=()):
        self.w = None
        self.r = list(seed)

    def ww(self):
        return self.r + ([self.w] if self.w else [])

    def wrote(self, tok):
        if tok is not None:
            self.w = tok
            self.r = []

    def rw(self):
        return [self.w] if self.w else []

    def read(self, tok):
        if tok is not None:
            self.r.append(tok)


def build(mode="full"):
    nc = bass.Bass("TRN2", target_bir_lowering=False)

    xh_d = nc.dram_tensor("xh", [RH, D], F32, kind="ExternalInput").ap()
    wq_d = nc.dram_tensor("wqT", [128, 8, D], F32, kind="ExternalInput").ap()
    wk_d = nc.dram_tensor("wkT", [128, 8, D], F32, kind="ExternalInput").ap()
    wv_d = nc.dram_tensor("wvT", [128, 8, D], F32, kind="ExternalInput").ap()
    wo_d = nc.dram_tensor("woT", [128, 8, D], F32, kind="ExternalInput").ap()
    w1_d = nc.dram_tensor("w1t", [32, 128, 1024], F32, kind="ExternalInput").ap()
    w2_d = nc.dram_tensor("w2t", [32, 128, 1024], F32, kind="ExternalInput").ap()
    h1_d = nc.dram_tensor("h1t", [32, 128, 4096], F32, kind="ExternalInput").ap()
    h2_d = nc.dram_tensor("h2t", [32, 128, 4096], F32, kind="ExternalInput").ap()
    w3_d = nc.dram_tensor("w3t", [8, 128, 4096], F32, kind="ExternalInput").ap()
    bq_d = nc.dram_tensor("bqr", [1, D], F32, kind="ExternalInput").ap()
    bk_d = nc.dram_tensor("bkr", [1, D], F32, kind="ExternalInput").ap()
    bv_d = nc.dram_tensor("bvr", [1, D], F32, kind="ExternalInput").ap()
    bo_d = nc.dram_tensor("bor", [1, D], F32, kind="ExternalInput").ap()
    s1_d = nc.dram_tensor("s1v", [128, 32], F32, kind="ExternalInput").ap()
    s2_d = nc.dram_tensor("s2v", [128, 32], F32, kind="ExternalInput").ap()
    sh1_d = nc.dram_tensor("sh1v", [128, 32], F32, kind="ExternalInput").ap()
    sh2_d = nc.dram_tensor("sh2v", [128, 32], F32, kind="ExternalInput").ap()
    s3_d = nc.dram_tensor("s3v", [128, 8], F32, kind="ExternalInput").ap()
    mb_d = nc.dram_tensor("maskb", [128, NW * NJC], F32, kind="ExternalInput").ap()
    tri_d = nc.dram_tensor("tri01", [128, 2, 128], F32, kind="ExternalInput").ap()
    id_d = nc.dram_tensor("ident", [128, 128], F32, kind="ExternalInput").ap()
    out_d = nc.dram_tensor("out", [R, D], F32, kind="ExternalOutput").ap()
    x2s_d = nc.dram_tensor("x2s", [R, D], F32).ap()

    em = Emitter()
    ct = Ctr("ct", "t")
    cs = Ctr("cs", "s")
    cv = Ctr("cv", "v")
    cgm = Ctr("cgm", "g")
    cp = Ctr("cp")
    ctri = Ctr("ctri")
    cwb = Ctr("cwb")
    ch = [Ctr("ch0"), Ctr("ch1")]
    cl = [Ctr("cl0"), Ctr("cl1"), Ctr("cl2")]
    cst = [Ctr("cst0"), Ctr("cst1")]
    ctrs = [ct, cs, cv, cgm, cp, ctri, cwb] + ch + cl + cst

    alloc = []

    def sb(name, shape, dtype):
        cm = nc.sbuf_tensor(name, shape, dtype)
        h = cm.__enter__()
        alloc.append(cm)
        return h

    def psum(name, shape, dtype):
        cm = nc.psum_tensor(name, shape, dtype)
        h = cm.__enter__()
        alloc.append(cm)
        return h

    def free_to(n):
        while len(alloc) > n:
            alloc.pop().__exit__(None, None, None)

    def gdma(ctr, dst, src, waits=()):
        return em.op("g", lambda g, d=dst, s=src: g.dma_start(d, s), waits=waits,
                     inc=ctr, inck=16)

    def ydma(ctr, dst, src, waits=()):
        return em.op("y", lambda y, d=dst, s=src: y.dma_start(d, s), waits=waits,
                     inc=ctr, inck=16)

    ones_bf = sb("ones_bf", [128, 512], BF16)
    ones_f = sb("ones_f", [128, 128], F32)
    ident = sb("identt", [128, 128], BF16)
    kpad = sb("kpad", [128, 128], BF16)
    consts = sb("consts", [128, 2], F32)
    bq_s = sb("bq_s", [1, D], BF16)
    bk_s = sb("bk_s", [1, D], BF16)
    mb_s = sb("mb_s", [128, NW * NJC], F32)
    bo_s = sb("bo_s", [1, D], BF16)
    bv_s = sb("bv_s", [1, D], BF16)
    s1_s = sb("s1_s", [128, 32], F32)
    s2_s = sb("s2_s", [128, 32], F32)
    sh1_s = sb("sh1_s", [128, 32], F32)
    sh2_s = sb("sh2_s", [128, 32], F32)
    s3_s = sb("s3_s", [128, 8], F32)
    EPS = consts[:, 0:1]
    INVN = consts[:, 1:2]
    persist_mark = len(alloc)

    em.op("g", lambda g: g.memset(ones_bf[:], 1.0))
    em.op("g", lambda g: g.memset(ones_f[:], 1.0))
    em.op("g", lambda g: g.memset(kpad[:], -1.0))
    em.op("g", lambda g: g.memset(consts[:, 0:1], 1e-5))
    t_gm = em.op("g", lambda g: g.memset(consts[:, 1:2], 1.0 / 1024.0), inc=cgm)

    for dst, src in [(bq_s[:], bq_d), (bk_s[:], bk_d), (mb_s[:], mb_d),
                     (bo_s[:], bo_d), (bv_s[:], bv_d), (s1_s[:], s1_d),
                     (s2_s[:], s2_d), (sh1_s[:], sh1_d), (sh2_s[:], sh2_d),
                     (s3_s[:], s3_d), (ident[:], id_d)]:
        gdma(cp, dst, src)

    xln_own = sb("xln_own", [128, 8, R], BF16)
    k_fm = sb("k_fm", [128, 8, RH], BF16)
    v_tm = sb("v_tm", [128, RH // 128, H * 65], BF16)
    vpad = sb("vpad", [128, H * 65], BF16)
    wbuf = sb("wbuf", [128, 8, D], BF16)
    attn_mark = len(alloc)

    em.op("g", lambda g: g.memset(
        v_tm[:].rearrange("p c (h u) -> p c h u", u=65)[:, :, :, 64:65], 1.0))
    em.op("g", lambda g: g.memset(
        vpad[:].rearrange("p (h u) -> p h u", u=65)[:, :, 0:64], -512.0))
    t_vones = em.op("g", lambda g: g.memset(
        vpad[:].rearrange("p (h u) -> p h u", u=65)[:, :, 64:65], 512.0), inc=cgm)

    wbuf2 = sb("wbuf2", [128, 8, D], BF16)
    xbuf = sb("xbuf", [128, 3, D], F32)
    lntmp = sb("lntmp", [128, 24], F32)
    xln_tm = sb("xln_tm", [128, 3, D], BF16)
    xln_L = sb("xln_L", [128, 8, HALO], BF16)
    xln_R = sb("xln_R", [128, 8, HALO], BF16)
    kv_mark = len(alloc)

    ps_tr = psum("ps_tr", [128, 1024], BF16)
    ps_p1 = psum("ps_p1", [128, 1024], F32)
    ps_p2 = psum("ps_p2", [128, 1024], F32)
    ps_p3 = psum("ps_p3", [128, 1024], F32)
    ps_bc = psum("ps_bc", [128, 512], F32)
    psum_mark = len(alloc)

    sl_xbuf = [Slot() for _ in range(3)]
    sl_xlntm = [Slot() for _ in range(3)]
    sl_pstr = Slot()
    sl_ln = [Slot() for _ in range(3)]
    sl_p1 = Slot()
    sl_p2 = Slot()
    sl_p3 = Slot()
    sl_bc = Slot()
    sl_xld = [Slot() for _ in range(RH // 128)]
    sl_vch = [Slot() for _ in range(RH // 128)]
    tok_kblk = [None] * 6

    gdma(cp, wbuf[:], wk_d)
    gdma(cp, wbuf2[:], wv_d)
    tok_cp = (cp, cp.n)
    tok_wk = tok_cp
    tok_wv = tok_cp
    tok_small = tok_cp

    def xln_dest(tc):
        if tc < 4:
            return xln_L[:, :, 128 * tc:128 * (tc + 1)]
        if tc >= 20:
            return xln_R[:, :, 128 * (tc - 20):128 * (tc - 19)]
        return xln_own[:, :, 128 * (tc - 4):128 * (tc - 3)]

    def xln_block(nt):
        if nt == 0:
            return xln_L
        if nt == 5:
            return xln_R
        return None

    def xln_blk_ap(nt, ic):
        if nt == 0:
            return xln_L[:, ic, :]
        if nt == 5:
            return xln_R[:, ic, :]
        return xln_own[:, ic, 512 * (nt - 1):512 * nt]

    def emit_ln_chain(dram_rows, pbuf, ln_slots, tmps, slx, xlntm_t,
                      extra_dma_waits=()):
        (NSUM, NMEAN, SUMSQ, MSQ, VAR, LNV, RSTD) = tmps[pbuf]
        sl_xb, sl_xt, sl_ps, sl_l2 = ln_slots
        sl_l = sl_l2[pbuf]
        xb = slx[:, pbuf, :]
        tokx = ydma(cl[pbuf % len(cl)] if pbuf < 3 else cl[0], xb, dram_rows,
                    waits=sl_xb[pbuf].ww() + list(extra_dma_waits))
        sl_xb[pbuf].wrote(tokx)
        tokv = em.op("v", lambda v: v.reduce_sum(NSUM, xb, axis=AX, negate=True),
                     waits=[tokx, t_gm] + sl_l.ww(), inc=cv)
        em.op("v", lambda v: v.tensor_scalar(NMEAN, NSUM, INVN, None, op0=ALU.mult),
              drain=True)
        toks = em.op("s", lambda s: s.activation(xlntm_t[:, pbuf, :], xb, AF.Square,
                                                 accum_out=SUMSQ),
                     waits=[tokx, tokv] + sl_xt[pbuf].ww(), inc=cs)
        em.op("v", lambda v: v.tensor_mul(MSQ, NMEAN, NMEAN), waits=[toks],
              drain=True)
        em.op("v", lambda v: v.tensor_scalar(VAR, SUMSQ, INVN, None, op0=ALU.mult))
        tokv2 = em.op("v", lambda v: v.tensor_sub(VAR, VAR, MSQ), inc=cv, drain=True)
        em.op("s", lambda s: s.activation(LNV, VAR, AF.Ln, bias=EPS), waits=[tokv2])
        toks2 = em.op("s", lambda s: s.activation(RSTD, LNV, AF.Exp, scale=-0.5),
                      inc=cs, drain=True)
        tokv3 = em.op("v", lambda v: v.tensor_scalar(xlntm_t[:, pbuf, :], xb, NMEAN,
                                                     RSTD, op0=ALU.add, op1=ALU.mult),
                      waits=[toks2, toks], inc=cv, drain=True)
        sl_l.wrote(tokv3)
        sl_xb[pbuf].read(tokv3)
        sl_xt[pbuf].wrote(tokv3)
        return tokv3

    def emit_ln_tr(dest_ap, dest_slot, pbuf, ln_slots, psx, xlntm_t, tokv3):
        sl_xb, sl_xt, sl_ps, sl_l2 = ln_slots
        tokt = None
        for c in range(8):
            tokt = em.op("t", lambda t, c=c: t.transpose(
                psx[:, 128 * c:128 * (c + 1)],
                xlntm_t[:, pbuf, 128 * c:128 * (c + 1)], ident[:]),
                waits=(sl_ps.ww() + [tokv3, tok_small] if c == 0 else []),
                inc=(ct if c == 7 else None))
        sl_xt[pbuf].read(tokt)
        sl_ps.wrote(tokt)
        tokev = em.op("v", lambda v: v.tensor_copy(
            dest_ap,
            psx[:].rearrange("p (c f) -> p c f", f=128)),
            waits=[tokt] + dest_slot.ww(), inc=cv)
        sl_ps.read(tokev)
        dest_slot.wrote(tokev)
        return tokev

    def emit_ln(dram_rows, dest_ap, dest_slot, pbuf, ln_slots, tmps, psx, slx,
                xlntm_t, extra_dma_waits=()):
        tokv3 = emit_ln_chain(dram_rows, pbuf, ln_slots, tmps, slx, xlntm_t,
                              extra_dma_waits)
        return emit_ln_tr(dest_ap, dest_slot, pbuf, ln_slots, psx, xlntm_t, tokv3)

    ln_slots = (sl_xbuf, sl_xlntm, sl_pstr, sl_ln)
    tmps = [tuple(lntmp[:, 8 * par + i:8 * par + i + 1] for i in range(7))
            for par in range(3)]

    ln1_pend = []
    for tc in range(RH // 128 + 2):
        if tc < RH // 128:
            tokv3 = emit_ln_chain(xh_d[128 * tc:128 * (tc + 1), :], tc % 3,
                                  ln_slots, tmps, xbuf, xln_tm)
            ln1_pend.append((tc, tokv3))
        if len(ln1_pend) < (2 if tc < RH // 128 else 1):
            continue
        tc, tokv3 = ln1_pend.pop(0)
        emit_ln_tr(xln_dest(tc), sl_xld[tc], tc % 3, ln_slots, ps_tr, xln_tm,
                   tokv3)
        xsrc = xln_dest(tc)
        tokm = None
        for ic in range(8):
            for on in range(2):
                tokm = em.op("t", lambda t, on=on, ic=ic, xsrc=xsrc: t.matmul(
                    ps_p1[:, 512 * on:512 * (on + 1)],
                    xsrc[:, ic, :],
                    wbuf2[:, ic, 512 * on:512 * (on + 1)],
                    start=(ic == 0), stop=False),
                    waits=(sl_p1.ww() + [sl_xld[tc].w, tok_wv]
                           if (on == 0 and ic == 0) else []))
        for on in range(2):
            tokm = em.op("t", lambda t, on=on: t.matmul(
                ps_p1[:, 512 * on:512 * (on + 1)], ones_bf[0:1, 0:128],
                bv_s[0:1, 512 * on:512 * (on + 1)], start=False, stop=True),
                inc=(ct if on == 1 else None))
        sl_p1.wrote(tokm)
        sl_xld[tc].read(tokm)
        toke = em.op("s", lambda s, tc=tc: s.activation(
            v_tm[:, tc, :].rearrange("p (h u) -> p h u", u=65)[:, :, 0:64],
            ps_p1[:].rearrange("p (h u) -> p h u", u=64), AF.Copy),
            waits=[tokm] + sl_vch[tc].ww(), inc=cs)
        sl_p1.read(toke)
        sl_vch[tc].wrote(toke)
        if tc % 4 == 3:
            nt = tc // 4
            blk_deps = [sl_xld[j].w for j in range(4 * nt, 4 * nt + 4)]
            toke_k = None
            for oc in range(8):
                pk = ps_p2 if oc % 2 == 0 else ps_p3
                slk = sl_p2 if oc % 2 == 0 else sl_p3
                half = (oc // 2) % 2
                tokmk = None
                for ic in range(8):
                    em.op("t", lambda t, oc=oc, ic=ic, half=half, nt=nt,
                          pk=pk: t.matmul(
                        pk[:, 512 * half:512 * (half + 1)],
                        wbuf[:, ic, 128 * oc:128 * (oc + 1)],
                        xln_blk_ap(nt, ic),
                        start=(ic == 0), stop=False),
                        waits=(slk.ww() + blk_deps + [tok_wk] if ic == 0 else []))
                tokmk = em.op("t", lambda t, oc=oc, half=half, pk=pk: t.matmul(
                    pk[:, 512 * half:512 * (half + 1)],
                    bk_s[0:1, 128 * oc:128 * (oc + 1)], ones_bf[0:1, 0:512],
                    start=False, stop=True), inc=ct)
                slk.wrote(tokmk)
                toke_k = em.op("s", lambda s, oc=oc, nt=nt, half=half, pk=pk:
                               s.activation(
                    k_fm[:, oc, 512 * nt:512 * (nt + 1)],
                    pk[:, 512 * half:512 * (half + 1)],
                    AF.Copy),
                    waits=[tokmk], inc=cs)
                slk.read(toke_k)
            tok_kblk[nt] = toke_k

    tok_kv_done = (ct, ct.n)

    free_to(attn_mark)
    tri_s = sb("tri_s", [128, 2, 128], BF16)
    q_fm = sb("q_fm", [128, 8, 512], BF16)
    probsT = sb("probsT", [128, 3, 1024], BF16)
    attn_un = sb("attn_un", [128, 512], F32)
    attn_fm = sb("attn_fm", [128, 2, 8, 512], BF16)
    rsum_sb = sb("rsum_sb", [128, 1024], F32)
    x2buf = sb("x2buf", [128, 1, D], F32)
    xbuf = sb("xbuf_w", [128, 2, D], F32)[:].tensor if False else sb("xbuf_w", [128, 1, D], F32)
    ps_scp = [psum("ps_scA", [128, 1024], F32), psum("ps_scB", [128, 1024], F32),
              psum("ps_scC", [128, 1024], F32)]
    ps_pv2 = psum("ps_pv2", [128, 1024], F32)

    bar = [(ct, ct.n), (cs, cs.n), (cv, cv.n)]
    sl_xbuf = [Slot(bar), Slot(bar)]
    sl_q = Slot(bar)
    sl_probs = [Slot(bar), Slot(bar), Slot(bar)]
    sl_attnun = [Slot(bar), Slot(bar)]
    sl_attnfm = [Slot(bar), Slot(bar)]
    sl_rsum = [Slot(bar), Slot(bar)]
    sl_x2buf = [Slot(bar), Slot(bar)]
    sl_wbuf = Slot(bar)
    sl_scp = [Slot(bar), Slot(bar), Slot(bar)]
    sl_pv = [Slot(bar), Slot(bar)]

    tok_tri = gdma(ctri, tri_s[:], tri_d,
                   waits=[tok_kv_done, (cs, cs.n), (cv, cv.n)])
    tok_w = gdma(cwb, wbuf[:], wq_d,
                 waits=[tok_kv_done, (cs, cs.n), (cv, cv.n)])
    sl_wbuf.wrote(tok_w)

    pend_tails = []
    attn_done_tok = [None, None]

    def emit_oproj_group(wo_, tc):
        wpar = wo_ % 2
        pb2 = tc % 2
        rows = xh_d[HALO + 512 * wo_ + 128 * tc: HALO + 512 * wo_ + 128 * (tc + 1), :]
        tokx = ydma(cl[pb2], xbuf[:, 0, :], rows, waits=sl_xbuf[0].ww())
        sl_xbuf[0].wrote(tokx)
        pko = ps_scp[2]
        pslo = sl_scp[2]
        tokmo = None
        for ic in range(8):
            for on in range(2):
                tokmo = em.op("t", lambda t, on=on, ic=ic, tc=tc, pko=pko,
                              wpar=wpar: t.matmul(
                    pko[:, 512 * on:512 * (on + 1)],
                    attn_fm[:, wpar, ic, 128 * tc:128 * (tc + 1)],
                    wbuf[:, ic, 512 * on:512 * (on + 1)],
                    start=(ic == 0), stop=False),
                    waits=(pslo.ww() + [attn_done_tok[wpar], sl_wbuf.w]
                           if (on == 0 and ic == 0) else []))
        for on in range(2):
            tokmo = em.op("t", lambda t, on=on, pko=pko: t.matmul(
                pko[:, 512 * on:512 * (on + 1)], ones_bf[0:1, 0:128],
                bo_s[0:1, 512 * on:512 * (on + 1)], start=False, stop=True),
                inc=(ct if on == 1 else None))
        pslo.wrote(tokmo)
        sl_wbuf.read(tokmo)
        if tc == 3:
            sl_attnfm[wpar].read(tokmo)
        toka = em.op("v", lambda v, pko=pko: v.tensor_add(
            x2buf[:, 0, :], pko[:, :], xbuf[:, 0, :]),
            waits=[tokmo, tokx] + sl_x2buf[0].ww(), inc=cv)
        pslo.read(toka)
        sl_xbuf[0].read(toka)
        sl_x2buf[0].wrote(toka)
        drows = (x2s_d if mode == "full" else out_d)[
            512 * wo_ + 128 * tc: 512 * wo_ + 128 * (tc + 1), :]
        tokst = ydma(cst[pb2], drows, x2buf[:, 0, :], waits=[toka])
        sl_x2buf[0].read(tokst)

    for w in range(NW):
        toke_q = None
        for oc in range(8):
            half = oc % 2
            pq = ps_scp[half]
            slq = sl_scp[half]
            for ic in range(8):
                em.op("t", lambda t, oc=oc, ic=ic, w=w, pq=pq: t.matmul(
                    pq[:, 0:512],
                    wbuf[:, ic, 128 * oc:128 * (oc + 1)],
                    xln_own[:, ic, 512 * w:512 * (w + 1)],
                    start=(ic == 0), stop=False),
                    waits=(slq.ww() + [sl_wbuf.w] if ic == 0 else []))
            tokmq = em.op("t", lambda t, oc=oc, pq=pq: t.matmul(
                pq[:, 0:512],
                bq_s[0:1, 128 * oc:128 * (oc + 1)], ones_bf[0:1, 0:512],
                start=False, stop=True), inc=ct)
            slq.wrote(tokmq)
            toke_q = em.op("s", lambda s, oc=oc, pq=pq: s.activation(
                q_fm[:, oc, :], pq[:, 0:512],
                AF.Copy),
                waits=[tokmq] + (sl_q.ww() if oc == 0 else []), inc=cs)
            slq.read(toke_q)
            sl_wbuf.read(tokmq)
        sl_q.wrote(toke_q)

        tok_w = gdma(cwb, wbuf[:], wo_d, waits=sl_wbuf.ww())
        sl_wbuf.wrote(tok_w)

        CORDER = [4, 5, 6, 7, 12, 0, 1, 2, 3, 8, 9, 10, 11]
        for hc in range(H // 2):
            exp_toks = [None] * 13
            sc_toks = [None] * 13

            def emit_scores(ci, pb, hc=hc, w=w, sc_toks=sc_toks):
                jc = CORDER[ci]
                pps = ps_scp[pb]
                off, wid = _band(jc)
                if jc < 12:
                    kof = 512 * w + 128 * jc
                    kdep = tok_kblk[kof // 512]
                else:
                    kdep = t_gm
                tokm = None
                for hp in range(2):
                    hoff = 64 * hp
                    if jc < 12:
                        klhs = k_fm[hoff:hoff + 64, hc, kof:kof + 128]
                    else:
                        klhs = kpad[hoff:hoff + 64, :]
                    tokm = em.op("t", lambda t, klhs=klhs, hp=hp, pps=pps,
                                 hoff=hoff, hc=hc, off=off, wid=wid: t.matmul(
                        pps[:, 512 * hp + off:512 * hp + off + wid],
                        klhs, q_fm[hoff:hoff + 64, hc, off:off + wid],
                        start=True, stop=True),
                        waits=(sl_scp[pb].ww() + [sl_q.w, kdep]) if hp == 0 else [],
                        inc=(ct if hp == 1 else None))
                sl_scp[pb].wrote(tokm)
                sl_q.read(tokm)
                sc_toks[ci] = tokm

            def emit_exp(ci, pb, w=w, sc_toks=sc_toks, exp_toks=exp_toks):
                jc = CORDER[ci]
                pps = ps_scp[pb]
                off, wid = _band(jc)
                ws_ = [sc_toks[ci], tok_small] + sl_probs[pb].ww()
                toke = None
                if wid == 512:
                    toke = em.op("s", lambda s, pb=pb, pps=pps, jc=jc, w=w:
                                 s.activation(
                        probsT[:, pb, 0:1024], pps[:, 0:1024],
                        AF.Exp, bias=mb_s[:, w * NJC + jc:w * NJC + jc + 1]),
                        waits=ws_, inc=cs)
                else:
                    for hp in range(2):
                        lo = 512 * hp + off
                        toke = em.op("s", lambda s, pb=pb, pps=pps, jc=jc, w=w,
                                     lo=lo, wid=wid: s.activation(
                            probsT[:, pb, lo:lo + wid], pps[:, lo:lo + wid],
                            AF.Exp, bias=mb_s[:, w * NJC + jc:w * NJC + jc + 1]),
                            waits=ws_, inc=cs)
                        ws_ = []
                sl_scp[pb].read(toke)
                sl_probs[pb].wrote(toke)
                ptok = toke
                if jc < 4:
                    tlo, tt = 128 * jc, 0
                elif 8 <= jc < 12:
                    tlo, tt = 128 * (jc - 8), 1
                else:
                    tlo = None
                if tlo is not None:
                    for hp in range(2):
                        lo = 512 * hp + tlo
                        ptok = em.op("v", lambda v, pb=pb, lo=lo, tt=tt:
                                     v.tensor_mul(
                            probsT[:, pb, lo:lo + 128], probsT[:, pb, lo:lo + 128],
                            tri_s[:, tt, :]),
                            waits=[toke, tok_tri], inc=cv)
                    sl_probs[pb].wrote(ptok)
                exp_toks[ci] = ptok

            def emit_pv(ci, pb, first_pv, last_pv, hc=hc, w=w,
                        exp_toks=exp_toks):
                jc = CORDER[ci]
                off, wid = _band(jc)
                tokpv = None
                for hp in range(2):
                    h = 2 * hc + hp
                    if jc < 12:
                        vl = v_tm[:, 4 * w + jc, :].rearrange(
                            "p (h u) -> p h u", u=65)[:, h, :]
                        vdep = sl_vch[4 * w + jc].w
                    else:
                        vl = vpad[:].rearrange("p (h u) -> p h u", u=65)[:, h, :]
                        vdep = t_vones
                    pvlo = 512 * hp
                    tokpv = em.op("t", lambda t, vl=vl, pb=pb, hp=hp, pvlo=pvlo,
                                  off=off, wid=wid, st_=first_pv, sp_=last_pv:
                                  t.matmul(
                        ps_pv2[0:65, pvlo + off:pvlo + off + wid],
                        vl, probsT[:, pb, 512 * hp + off:512 * hp + off + wid],
                        start=st_, stop=sp_),
                        waits=[exp_toks[ci], vdep]
                        + (sl_pv[hp].ww() if first_pv else []),
                        inc=(ct if (last_pv and hp == 1) else None))
                return tokpv

            for i0 in range(3):
                emit_scores(i0, i0 % 3)
                emit_exp(i0, i0 % 3)
                if pend_tails:
                    pend_tails.pop(0)()
            while pend_tails:
                pend_tails.pop(0)()
            for i in range(13):
                tokpv = emit_pv(i, i % 3, first_pv=(i == 0), last_pv=(i == 12))
                if i + 3 < 13:
                    emit_scores(i + 3, (i + 3) % 3)
                    emit_exp(i + 3, (i + 3) % 3)
            sl_pv[0].wrote(tokpv)
            sl_pv[1].wrote(tokpv)
            for sp in sl_probs:
                sp.read(tokpv)
            for hp in range(2):
                pvlo = 512 * hp
                hoff = 64 * hp
                tokr = em.op("v", lambda v, pvlo=pvlo: v.reciprocal(
                    rsum_sb[64:65, pvlo:pvlo + 512],
                    ps_pv2[64:65, pvlo:pvlo + 512]),
                    waits=[tokpv] + sl_rsum[hp].ww(), inc=cv)
                sl_rsum[hp].wrote(tokr)
                toke2 = em.op("s", lambda s, pvlo=pvlo, hoff=hoff: s.activation(
                    attn_un[hoff:hoff + 64, :], ps_pv2[0:64, pvlo:pvlo + 512],
                    AF.Copy),
                    waits=[tokpv] + sl_attnun[hp].ww(), inc=cs)
                sl_attnun[hp].wrote(toke2)
                sl_pv[hp].read(tokr)
                sl_pv[hp].read(toke2)

                def mk_tail(hp=hp, tokr=tokr, toke2=toke2, hc=hc, wpar=w % 2):
                    def run():
                        hoff = 64 * hp
                        pvlo = 512 * hp
                        tokb = em.op("t", lambda t: t.matmul(
                            ps_pv2[:, pvlo:pvlo + 512], ones_f[64:65, :],
                            rsum_sb[64:65, pvlo:pvlo + 512],
                            start=True, stop=True),
                            waits=[tokr, toke2, t_gm] + sl_pv[hp].ww(), inc=ct)
                        sl_rsum[hp].read(tokb)
                        sl_pv[hp].wrote(tokb)
                        tokn = em.op("v", lambda v: v.tensor_mul(
                            attn_fm[hoff:hoff + 64, wpar, hc, :],
                            attn_un[hoff:hoff + 64, :],
                            ps_pv2[hoff:hoff + 64, pvlo:pvlo + 512]),
                            waits=[tokb, toke2]
                            + (sl_attnfm[wpar].ww() if hc == 0 else []), inc=cv)
                        sl_attnun[hp].read(tokn)
                        sl_pv[hp].read(tokn)
                        if hc == H // 2 - 1 and hp == 1:
                            sl_attnfm[wpar].wrote(tokn)
                            attn_done_tok[wpar] = tokn
                    return run
                pend_tails.append(mk_tail())
            if w >= 1 and hc in (2, 4, 6, 7):
                emit_oproj_group(w - 1, {2: 0, 4: 1, 6: 2, 7: 3}[hc])

        if w < NW - 1:
            tok_w = gdma(cwb, wbuf[:], wq_d, waits=sl_wbuf.ww())
            sl_wbuf.wrote(tok_w)

    while pend_tails:
        pend_tails.pop(0)()
    for tc in range(4):
        emit_oproj_group(NW - 1, tc)

    if mode == "attn":
        _finalize(nc, em, ctrs, [(cst[0], cst[0].n), (cst[1], cst[1].n)])
        return nc

    free_to(persist_mark)
    bar = [(ct, ct.n), (cs, cs.n), (cv, cv.n),
           (cst[0], cst[0].n), (cst[1], cst[1].n),
           (cl[0], cl[0].n), (cl[1], cl[1].n), (cwb, cwb.n), (ctri, ctri.n)]

    ps_tr2 = psum("ps_tr2", [128, 1024], BF16)
    ps_fa = psum("ps_fa", [128, 1024], F32)
    ps_fb = psum("ps_fb", [128, 1024], F32)

    x2in = sb("x2in", [128, 2, D], F32)
    lntmp2 = sb("lntmp2", [128, 16], F32)
    xln2_tm = sb("xln2_tm", [128, 2, D], BF16)
    xln2_fm = sb("xln2_fm", [128, 8, TP], BF16)
    y_fm = sb("y_fm", [128, 8, TP], BF16)
    a_fm = sb("a_fm", [128, 32, TP], BF16)
    b_fm = sb("b_fm", [128, 32, TP], BF16)
    hcol = sb("hcol", [128, 2, 4096], BF16)
    t1_sb = sb("t1_sb", [128, TP], BF16)
    osb = sb("osb", [128, 1, D], F32)

    tmps2 = [tuple(lntmp2[:, 8 * par + i:8 * par + i + 1] for i in range(7))
             for par in range(2)]

    sl_x2in = [Slot(bar), Slot(bar)]
    sl_x2tm = [Slot(bar), Slot(bar)]
    sl_ln2 = [Slot(bar), Slot(bar)]
    sl_ps2 = Slot(bar)
    sl_xln2 = Slot(bar)
    sl_afm = Slot(bar)
    sl_bfm = Slot(bar)
    sl_hcol = [Slot(bar), Slot(bar)]
    sl_t1 = Slot(bar)
    sl_fa = Slot(bar)
    sl_fb = Slot(bar)
    sl_osb = [Slot(bar), Slot(bar)]
    sl_yfm = Slot(bar)
    ln2_slots = (sl_x2in, sl_x2tm, sl_ps2, sl_ln2)

    x2_stores = [(cst[0], cst[0].n), (cst[1], cst[1].n)]

    def emit_ln2(p):
        tokev = None
        pend = []
        for tc in range(TP // 128 + 2):
            if tc < TP // 128:
                row0 = TP * p + 128 * tc
                tokv3 = emit_ln_chain(x2s_d[row0:row0 + 128, :], tc % 2,
                                      ln2_slots, tmps2, x2in, xln2_tm,
                                      extra_dma_waits=x2_stores)
                pend.append((tc, tokv3))
            if len(pend) < (2 if tc < TP // 128 else 1):
                continue
            tcp, tokv3 = pend.pop(0)
            tokev = emit_ln_tr(xln2_fm[:, :, 128 * tcp:128 * (tcp + 1)],
                               sl_xln2 if tcp == TP // 128 - 1 else Slot(),
                               tcp % 2, ln2_slots, ps_tr2, xln2_tm, tokv3)
        sl_xln2.wrote(tokev)

    emit_ln2(0)
    for p in range(NPASS):
        tokav = None
        for oc in range(32):
            hb = oc % 2
            tokw1 = gdma(ch[hb], hcol[:, hb, 0:1024], w1_d[oc],
                         waits=sl_hcol[hb].ww())
            tokw2 = gdma(ch[hb], hcol[:, hb, 1024:2048], w2_d[oc])
            sl_hcol[hb].wrote(tokw2)
            tokm1 = None
            for nt in range(TP // 512):
                for ic in range(8):
                    tokm1 = em.op("t", lambda t, ic=ic, nt=nt, hb=hb: t.matmul(
                        ps_fa[:, 512 * nt:512 * (nt + 1)],
                        hcol[:, hb, 128 * ic:128 * (ic + 1)],
                        xln2_fm[:, ic, 512 * nt:512 * (nt + 1)],
                        start=(ic == 0), stop=(ic == 7)),
                        waits=(sl_fa.ww() + [tokw2, sl_xln2.w]
                               if (nt == 0 and ic == 0) else []),
                        inc=(ct if (nt == TP // 512 - 1 and ic == 7) else None))
            sl_fa.wrote(tokm1)
            tokm2 = None
            for nt in range(TP // 512):
                for ic in range(8):
                    tokm2 = em.op("t", lambda t, ic=ic, nt=nt, hb=hb: t.matmul(
                        ps_fb[:, 512 * nt:512 * (nt + 1)],
                        hcol[:, hb, 1024 + 128 * ic:1024 + 128 * (ic + 1)],
                        xln2_fm[:, ic, 512 * nt:512 * (nt + 1)],
                        start=(ic == 0), stop=(ic == 7)),
                        waits=(sl_fb.ww() if (nt == 0 and ic == 0) else []),
                        inc=(ct if (nt == TP // 512 - 1 and ic == 7) else None))
            sl_fb.wrote(tokm2)
            sl_hcol[hb].read(tokm2)
            sl_xln2.read(tokm2)
            toksl = em.op("s", lambda s, oc=oc: s.activation(
                t1_sb[:], ps_fa[:, :], AF.Silu, scale=s1_s[:, oc:oc + 1]),
                waits=[tokm1] + sl_t1.ww(), inc=cs)
            sl_fa.read(toksl)
            sl_t1.wrote(toksl)
            tokav = em.op("v", lambda v, oc=oc: v.scalar_tensor_tensor(
                a_fm[:, oc, :], ps_fb[:, :], s2_s[:, oc:oc + 1], t1_sb[:],
                op0=ALU.mult, op1=ALU.mult),
                waits=[tokm2, toksl] + (sl_afm.ww() if oc == 0 else []), inc=cv)
            sl_fb.read(tokav)
            sl_t1.read(tokav)
        sl_afm.wrote(tokav)

        def big_stage(src_fm, src_slot, dst_fm, dst_slot, w_dram, sc_ap, nob, act,
                      nic, filler=None):
            tokl = None
            for oc in range(nob):
                if filler is not None:
                    filler(oc)
                hb = oc % 2
                pso = ps_fa if oc % 2 == 0 else ps_fb
                slo = sl_fa if oc % 2 == 0 else sl_fb
                tokw = gdma(ch[hb], hcol[:, hb, 0:128 * nic],
                            w_dram[oc][:, 0:128 * nic], waits=sl_hcol[hb].ww())
                sl_hcol[hb].wrote(tokw)
                tokm = None
                for nt in range(TP // 512):
                    for ic in range(nic):
                        tokm = em.op("t", lambda t, ic=ic, nt=nt, hb=hb, pso=pso:
                                     t.matmul(
                            pso[:, 512 * nt:512 * (nt + 1)],
                            hcol[:, hb, 128 * ic:128 * (ic + 1)],
                            src_fm[:, ic, 512 * nt:512 * (nt + 1)],
                            start=(ic == 0), stop=(ic == nic - 1)),
                            waits=(slo.ww() + [tokw, src_slot.w]
                                   if (nt == 0 and ic == 0) else []),
                            inc=(ct if (nt == TP // 512 - 1 and ic == nic - 1)
                                 else None))
                slo.wrote(tokm)
                sl_hcol[hb].read(tokm)
                src_slot.read(tokm)
                tokl = em.op("s", lambda s, oc=oc, pso=pso: s.activation(
                    dst_fm[:, oc, :], pso[:, :], act, scale=sc_ap[:, oc:oc + 1]),
                    waits=[tokm] + (dst_slot.ww() if oc == 0 else []), inc=cs)
                slo.read(tokl)
            dst_slot.wrote(tokl)
            return tokl

        if p + 1 < NPASS:
            ln2_state = [0, None]

            def ln2_filler(oc, p=p):
                tc_n, pend = ln2_state
                if oc % 2 == 1 and pend is not None:
                    tc, tokv3 = pend
                    tokev = emit_ln_tr(xln2_fm[:, :, 128 * tc:128 * (tc + 1)],
                                       sl_xln2 if tc == TP // 128 - 1 else Slot(),
                                       tc % 2, ln2_slots, ps_tr2, xln2_tm, tokv3)
                    if tc == TP // 128 - 1:
                        sl_xln2.wrote(tokev)
                    ln2_state[1] = None
                elif oc % 2 == 0 and tc_n < TP // 128 and pend is None:
                    row0 = TP * (p + 1) + 128 * tc_n
                    tokv3 = emit_ln_chain(x2s_d[row0:row0 + 128, :], tc_n % 2,
                                          ln2_slots, tmps2, x2in, xln2_tm,
                                          extra_dma_waits=x2_stores)
                    ln2_state[0] = tc_n + 1
                    ln2_state[1] = (tc_n, tokv3)
        else:
            ln2_filler = None
        big_stage(a_fm, sl_afm, b_fm, sl_bfm, h1_d, sh1_s, 32, AF.Gelu, 32,
                  filler=ln2_filler)
        big_stage(b_fm, sl_bfm, a_fm, sl_afm, h2_d, sh2_s, 32, AF.Gelu, 32)
        toky = big_stage(a_fm, sl_afm, y_fm, sl_yfm, w3_d, s3_s, 8, AF.Copy, 32)

        for tc in range(TP // 128):
            row0 = TP * p + 128 * tc
            pb2 = tc % 2
            tokx = ydma(cl[pb2], x2in[:, pb2, :], x2s_d[row0:row0 + 128, :],
                        waits=sl_x2in[pb2].ww() + x2_stores)
            sl_x2in[pb2].wrote(tokx)
            tokt = None
            for c in range(8):
                tokt = em.op("t", lambda t, c=c, tc=tc: t.transpose(
                    ps_tr2[:, 128 * c:128 * (c + 1)],
                    y_fm[:, c, 128 * tc:128 * (tc + 1)], ident[:]),
                    waits=(sl_ps2.ww() + [toky] if c == 0 else []),
                    inc=(ct if c == 7 else None))
            sl_ps2.wrote(tokt)
            sl_yfm.read(tokt)
            toko = em.op("v", lambda v, pb2=pb2: v.tensor_add(
                osb[:, 0, :], ps_tr2[:, :], x2in[:, pb2, :]),
                waits=[tokt, tokx] + sl_osb[0].ww(), inc=cv)
            sl_ps2.read(toko)
            sl_x2in[pb2].read(toko)
            sl_osb[0].wrote(toko)
            tokst = ydma(cst[pb2], out_d[row0:row0 + 128, :], osb[:, 0, :],
                         waits=[toko])
            sl_osb[0].read(tokst)

    _finalize(nc, em, ctrs, [(cst[0], cst[0].n), (cst[1], cst[1].n)])
    return nc


def _finalize(nc, em, ctrs, final_tok):
    import contextlib
    with contextlib.ExitStack() as st:
        sems = {c.name: st.enter_context(nc.semaphore(c.name)) for c in ctrs}
        with nc.Block() as block:

            @block.tensor
            def _(eng):
                em.replay("t", eng, sems)

            @block.scalar
            def _(eng):
                em.replay("s", eng, sems)

            @block.vector
            def _(eng):
                em.replay("v", eng, sems)

            @block.gpsimd
            def _(eng):
                em.replay("g", eng, sems)

            @block.sync
            def _(eng):
                em.replay("y", eng, sems)
                for c, v in final_tok:
                    eng.wait_ge(sems[c.name], v)


def _prep_shared(L0_attn_wq, L0_attn_bq, L0_attn_wk, L0_attn_bk, L0_attn_wv,
                 L0_attn_bv, L0_attn_wo, L0_attn_bo, L0_ffn_w1_w, L0_ffn_w1_s,
                 L0_ffn_w2_w, L0_ffn_w2_s, L0_ffn_h1_w, L0_ffn_h1_s, L0_ffn_h2_w,
                 L0_ffn_h2_s, L0_ffn_w3_w, L0_ffn_w3_s):
    f = np.float32
    c = np.ascontiguousarray

    def wt_qkvo(w):
        return c(np.asarray(w, f).T.reshape(8, 128, D).transpose(1, 0, 2))

    def tile_in(wT, nic, noc):
        t = np.asarray(wT, f).reshape(nic, 128, noc, 128).transpose(2, 1, 0, 3)
        return c(t.reshape(noc, 128, nic * 128))

    d = {
        "wqT": wt_qkvo(L0_attn_wq), "wkT": wt_qkvo(L0_attn_wk),
        "wvT": wt_qkvo(L0_attn_wv), "woT": wt_qkvo(L0_attn_wo),
        "w1t": tile_in(np.asarray(L0_ffn_w1_w, f).T, 8, 32),
        "w2t": tile_in(np.asarray(L0_ffn_w2_w, f).T, 8, 32),
        "h1t": tile_in(np.asarray(L0_ffn_h1_w, f).T, 32, 32),
        "h2t": tile_in(np.asarray(L0_ffn_h2_w, f).T, 32, 32),
        "w3t": tile_in(np.asarray(L0_ffn_w3_w, f).T, 32, 8),
        "bqr": np.asarray(L0_attn_bq, f).reshape(1, D).copy(),
        "bkr": np.asarray(L0_attn_bk, f).reshape(1, D).copy(),
        "bvr": np.asarray(L0_attn_bv, f).reshape(1, D).copy(),
        "bor": np.asarray(L0_attn_bo, f).reshape(1, D).copy(),
        "s1v": c(np.asarray(L0_ffn_w1_s, f).reshape(32, 128).T),
        "s2v": c(np.asarray(L0_ffn_w2_s, f).reshape(32, 128).T),
        "sh1v": c(np.asarray(L0_ffn_h1_s, f).reshape(32, 128).T),
        "sh2v": c(np.asarray(L0_ffn_h2_s, f).reshape(32, 128).T),
        "s3v": c(np.asarray(L0_ffn_w3_s, f).reshape(8, 128).T),
        "ident": np.eye(128, dtype=f),
    }
    pp = np.arange(128)[:, None]
    ii = np.arange(128)[None, :]
    tri = np.zeros((128, 2, 128), f)
    tri[:, 0, :] = (ii <= pp)
    tri[:, 1, :] = (ii >= pp)
    d["tri01"] = tri
    return d


def _prep_core(x, core):
    f = np.float32
    b = core // 4
    T0 = (core % 4) * R
    xh = np.zeros((RH, D), f)
    lo = max(0, T0 - HALO)
    hi = min(S_SEQ, T0 + R + HALO)
    xh[lo - (T0 - HALO): hi - (T0 - HALO), :] = np.asarray(x[b, lo:hi, :], f)
    mb = np.empty((128, NW * NJC), f)
    for w in range(NW):
        for jc in range(12):
            g = T0 - HALO + 512 * w + 128 * jc + np.arange(128)
            real = (g >= 0) & (g < S_SEQ)
            mb[:, w * NJC + jc] = np.where(real, SHIFT, SHIFT + MASKV)
        edge = (T0 + 512 * w == 0) or (T0 + 512 * w + 512 == S_SEQ)
        col = np.full(128, SHIFT + MASKV, f)
        if edge:
            col[0] = SHIFT
        mb[:, w * NJC + 12] = col
    return {"xh": xh, "maskb": mb}


def kernel(**inputs):
    mode = inputs.pop("_mode", "full")
    x = np.asarray(inputs.pop("L0_attn_x"), np.float32)
    shared = _prep_shared(**inputs)
    in_maps = []
    for core in range(N_CORES):
        m = dict(shared)
        m.update(_prep_core(x, core))
        in_maps.append(m)
    nc = build(mode)
    res = run_bass_kernel_spmd(nc, in_maps, core_ids=list(range(N_CORES)))
    parts = [res.results[core]["out"] for core in range(N_CORES)]
    out = np.empty((B_BATCH, S_SEQ, D), np.float32)
    for core in range(N_CORES):
        b = core // 4
        T0 = (core % 4) * R
        out[b, T0:T0 + R, :] = parts[core]
    return out


# version 24 (final)
# speedup vs baseline: 1.0476x; 1.0016x over previous
import numpy as np
import concourse.bass as bass
import concourse.mybir as mybir
from concourse.bass_utils import run_bass_kernel_spmd

F32 = mybir.dt.float32
BF16 = mybir.dt.bfloat16
AF = mybir.ActivationFunctionType
ALU = mybir.AluOpType
AX = mybir.AxisListType.X

D = 1024
H = 16
R = 2048
HALO = 512
RH = R + 2 * HALO
NW = 4
NJC = 13
DH = 4096
TP = 1024
NPASS = R // TP
SHIFT = -24.0
MASKV = -30000.0

N_CORES = 8
S_SEQ = 8192
B_BATCH = 2


def _band(jc):
    if jc < 4:
        return 0, 128 * (jc + 1)
    if jc < 8 or jc == 12:
        return 0, 512
    off = 128 * (jc - 8)
    return off, 512 - off


class Ctr:
    __slots__ = ("name", "eng", "n")

    def __init__(self, name, eng=None):
        self.name, self.eng, self.n = name, eng, 0


class Emitter:
    ENGS = ("t", "s", "v", "g", "y")

    def __init__(self):
        self.ops = {e: [] for e in self.ENGS}
        self.last_wait = {e: {} for e in self.ENGS}

    def op(self, eng, fn, waits=(), inc=None, inck=1, drain=False):
        wl = {}
        for tok in waits:
            if tok is None:
                continue
            c, v = tok
            if c.eng == eng or v <= 0:
                continue
            if wl.get(c, 0) < v:
                wl[c] = v
        resolved = []
        lw = self.last_wait[eng]
        for c, v in wl.items():
            if lw.get(c, 0) >= v:
                continue
            lw[c] = v
            resolved.append((c, v))
        tok = None
        if inc is not None:
            inc.n += inck
            tok = (inc, inc.n)
        self.ops[eng].append((resolved, fn, inc, inck, drain))
        return tok

    def replay(self, key, eng, sems):
        for waits, fn, inc, inck, drain in self.ops[key]:
            if drain:
                eng.drain()
            for c, v in waits:
                eng.wait_ge(sems[c.name], v)
            inst = fn(eng)
            if inc is not None:
                inst.then_inc(sems[inc.name], inck)


class Slot:
    __slots__ = ("w", "r")

    def __init__(self, seed=()):
        self.w = None
        self.r = list(seed)

    def ww(self):
        return self.r + ([self.w] if self.w else [])

    def wrote(self, tok):
        if tok is not None:
            self.w = tok
            self.r = []

    def rw(self):
        return [self.w] if self.w else []

    def read(self, tok):
        if tok is not None:
            self.r.append(tok)


def build(mode="full"):
    nc = bass.Bass("TRN2", target_bir_lowering=False)

    xh_d = nc.dram_tensor("xh", [RH, D], F32, kind="ExternalInput").ap()
    wq_d = nc.dram_tensor("wqT", [128, 8, D], F32, kind="ExternalInput").ap()
    wk_d = nc.dram_tensor("wkT", [128, 8, D], F32, kind="ExternalInput").ap()
    wv_d = nc.dram_tensor("wvT", [128, 8, D], F32, kind="ExternalInput").ap()
    wo_d = nc.dram_tensor("woT", [128, 8, D], F32, kind="ExternalInput").ap()
    w1_d = nc.dram_tensor("w1t", [32, 128, 1024], F32, kind="ExternalInput").ap()
    w2_d = nc.dram_tensor("w2t", [32, 128, 1024], F32, kind="ExternalInput").ap()
    h1_d = nc.dram_tensor("h1t", [32, 128, 4096], F32, kind="ExternalInput").ap()
    h2_d = nc.dram_tensor("h2t", [32, 128, 4096], F32, kind="ExternalInput").ap()
    w3_d = nc.dram_tensor("w3t", [8, 128, 4096], F32, kind="ExternalInput").ap()
    bq_d = nc.dram_tensor("bqr", [1, D], F32, kind="ExternalInput").ap()
    bk_d = nc.dram_tensor("bkr", [1, D], F32, kind="ExternalInput").ap()
    bv_d = nc.dram_tensor("bvr", [1, D], F32, kind="ExternalInput").ap()
    bo_d = nc.dram_tensor("bor", [1, D], F32, kind="ExternalInput").ap()
    s1_d = nc.dram_tensor("s1v", [128, 32], F32, kind="ExternalInput").ap()
    s2_d = nc.dram_tensor("s2v", [128, 32], F32, kind="ExternalInput").ap()
    sh1_d = nc.dram_tensor("sh1v", [128, 32], F32, kind="ExternalInput").ap()
    sh2_d = nc.dram_tensor("sh2v", [128, 32], F32, kind="ExternalInput").ap()
    s3_d = nc.dram_tensor("s3v", [128, 8], F32, kind="ExternalInput").ap()
    mb_d = nc.dram_tensor("maskb", [128, NW * NJC], F32, kind="ExternalInput").ap()
    tri_d = nc.dram_tensor("tri01", [128, 2, 128], F32, kind="ExternalInput").ap()
    id_d = nc.dram_tensor("ident", [128, 128], F32, kind="ExternalInput").ap()
    out_d = nc.dram_tensor("out", [R, D], F32, kind="ExternalOutput").ap()
    x2s_d = nc.dram_tensor("x2s", [R, D], F32).ap()

    em = Emitter()
    ct = Ctr("ct", "t")
    cs = Ctr("cs", "s")
    cv = Ctr("cv", "v")
    cgm = Ctr("cgm", "g")
    cp = Ctr("cp")
    ctri = Ctr("ctri")
    cwb = Ctr("cwb")
    ch = [Ctr("ch0"), Ctr("ch1")]
    cl = [Ctr("cl0"), Ctr("cl1"), Ctr("cl2")]
    cst = [Ctr("cst0"), Ctr("cst1")]
    ctrs = [ct, cs, cv, cgm, cp, ctri, cwb] + ch + cl + cst

    alloc = []

    def sb(name, shape, dtype):
        cm = nc.sbuf_tensor(name, shape, dtype)
        h = cm.__enter__()
        alloc.append(cm)
        return h

    def psum(name, shape, dtype):
        cm = nc.psum_tensor(name, shape, dtype)
        h = cm.__enter__()
        alloc.append(cm)
        return h

    def free_to(n):
        while len(alloc) > n:
            alloc.pop().__exit__(None, None, None)

    def gdma(ctr, dst, src, waits=()):
        return em.op("g", lambda g, d=dst, s=src: g.dma_start(d, s), waits=waits,
                     inc=ctr, inck=16)

    def ydma(ctr, dst, src, waits=()):
        return em.op("y", lambda y, d=dst, s=src: y.dma_start(d, s), waits=waits,
                     inc=ctr, inck=16)

    ones_bf = sb("ones_bf", [128, 512], BF16)
    ones_f = sb("ones_f", [128, 128], F32)
    ident = sb("identt", [128, 128], BF16)
    kpad = sb("kpad", [128, 128], BF16)
    consts = sb("consts", [128, 2], F32)
    bq_s = sb("bq_s", [1, D], BF16)
    bk_s = sb("bk_s", [1, D], BF16)
    mb_s = sb("mb_s", [128, NW * NJC], F32)
    bo_s = sb("bo_s", [1, D], BF16)
    bv_s = sb("bv_s", [1, D], BF16)
    s1_s = sb("s1_s", [128, 32], F32)
    s2_s = sb("s2_s", [128, 32], F32)
    sh1_s = sb("sh1_s", [128, 32], F32)
    sh2_s = sb("sh2_s", [128, 32], F32)
    s3_s = sb("s3_s", [128, 8], F32)
    EPS = consts[:, 0:1]
    INVN = consts[:, 1:2]
    persist_mark = len(alloc)

    em.op("g", lambda g: g.memset(ones_bf[:], 1.0))
    em.op("g", lambda g: g.memset(ones_f[:], 1.0))
    em.op("g", lambda g: g.memset(kpad[:], -1.0))
    em.op("g", lambda g: g.memset(consts[:, 0:1], 1e-5))
    t_gm = em.op("g", lambda g: g.memset(consts[:, 1:2], 1.0 / 1024.0), inc=cgm)

    for dst, src in [(bq_s[:], bq_d), (bk_s[:], bk_d), (mb_s[:], mb_d),
                     (bo_s[:], bo_d), (bv_s[:], bv_d), (s1_s[:], s1_d),
                     (s2_s[:], s2_d), (sh1_s[:], sh1_d), (sh2_s[:], sh2_d),
                     (s3_s[:], s3_d), (ident[:], id_d)]:
        gdma(cp, dst, src)

    xln_own = sb("xln_own", [128, 8, R], BF16)
    k_fm = sb("k_fm", [128, 8, RH], BF16)
    v_tm = sb("v_tm", [128, RH // 128, H * 65], BF16)
    vpad = sb("vpad", [128, H * 65], BF16)
    wbuf = sb("wbuf", [128, 8, D], BF16)
    attn_mark = len(alloc)

    em.op("g", lambda g: g.memset(
        v_tm[:].rearrange("p c (h u) -> p c h u", u=65)[:, :, :, 64:65], 1.0))
    em.op("g", lambda g: g.memset(
        vpad[:].rearrange("p (h u) -> p h u", u=65)[:, :, 0:64], -512.0))
    t_vones = em.op("g", lambda g: g.memset(
        vpad[:].rearrange("p (h u) -> p h u", u=65)[:, :, 64:65], 512.0), inc=cgm)

    wbuf2 = sb("wbuf2", [128, 8, D], BF16)
    xbuf = sb("xbuf", [128, 3, D], F32)
    lntmp = sb("lntmp", [128, 24], F32)
    xln_tm = sb("xln_tm", [128, 3, D], BF16)
    xln_L = sb("xln_L", [128, 8, HALO], BF16)
    xln_R = sb("xln_R", [128, 8, HALO], BF16)
    kv_mark = len(alloc)

    ps_tr = psum("ps_tr", [128, 1024], BF16)
    ps_p1 = psum("ps_p1", [128, 1024], F32)
    ps_p2 = psum("ps_p2", [128, 1024], F32)
    ps_p3 = psum("ps_p3", [128, 1024], F32)
    ps_bc = psum("ps_bc", [128, 512], F32)
    psum_mark = len(alloc)

    sl_xbuf = [Slot() for _ in range(3)]
    sl_xlntm = [Slot() for _ in range(3)]
    sl_pstr = Slot()
    sl_ln = [Slot() for _ in range(3)]
    sl_p1 = Slot()
    sl_p2 = Slot()
    sl_p3 = Slot()
    sl_bc = Slot()
    sl_xld = [Slot() for _ in range(RH // 128)]
    sl_vch = [Slot() for _ in range(RH // 128)]
    tok_kblk = [None] * 6

    gdma(cp, wbuf[:], wk_d)
    gdma(cp, wbuf2[:], wv_d)
    tok_cp = (cp, cp.n)
    tok_wk = tok_cp
    tok_wv = tok_cp
    tok_small = tok_cp

    def xln_dest(tc):
        if tc < 4:
            return xln_L[:, :, 128 * tc:128 * (tc + 1)]
        if tc >= 20:
            return xln_R[:, :, 128 * (tc - 20):128 * (tc - 19)]
        return xln_own[:, :, 128 * (tc - 4):128 * (tc - 3)]

    def xln_block(nt):
        if nt == 0:
            return xln_L
        if nt == 5:
            return xln_R
        return None

    def xln_blk_ap(nt, ic):
        if nt == 0:
            return xln_L[:, ic, :]
        if nt == 5:
            return xln_R[:, ic, :]
        return xln_own[:, ic, 512 * (nt - 1):512 * nt]

    def emit_ln_chain(dram_rows, pbuf, ln_slots, tmps, slx, xlntm_t,
                      extra_dma_waits=()):
        (NSUM, NMEAN, SUMSQ, MSQ, VAR, LNV, RSTD) = tmps[pbuf]
        sl_xb, sl_xt, sl_ps, sl_l2 = ln_slots
        sl_l = sl_l2[pbuf]
        xb = slx[:, pbuf, :]
        tokx = ydma(cl[pbuf % len(cl)] if pbuf < 3 else cl[0], xb, dram_rows,
                    waits=sl_xb[pbuf].ww() + list(extra_dma_waits))
        sl_xb[pbuf].wrote(tokx)
        tokv = em.op("v", lambda v: v.reduce_sum(NSUM, xb, axis=AX, negate=True),
                     waits=[tokx, t_gm] + sl_l.ww(), inc=cv)
        em.op("v", lambda v: v.tensor_scalar(NMEAN, NSUM, INVN, None, op0=ALU.mult),
              drain=True)
        toks = em.op("s", lambda s: s.activation(xlntm_t[:, pbuf, :], xb, AF.Square,
                                                 accum_out=SUMSQ),
                     waits=[tokx, tokv] + sl_xt[pbuf].ww(), inc=cs)
        em.op("v", lambda v: v.tensor_mul(MSQ, NMEAN, NMEAN), waits=[toks],
              drain=True)
        em.op("v", lambda v: v.tensor_scalar(VAR, SUMSQ, INVN, None, op0=ALU.mult))
        tokv2 = em.op("v", lambda v: v.tensor_sub(VAR, VAR, MSQ), inc=cv, drain=True)
        em.op("s", lambda s: s.activation(LNV, VAR, AF.Ln, bias=EPS), waits=[tokv2])
        toks2 = em.op("s", lambda s: s.activation(RSTD, LNV, AF.Exp, scale=-0.5),
                      inc=cs, drain=True)
        tokv3 = em.op("v", lambda v: v.tensor_scalar(xlntm_t[:, pbuf, :], xb, NMEAN,
                                                     RSTD, op0=ALU.add, op1=ALU.mult),
                      waits=[toks2, toks], inc=cv, drain=True)
        sl_l.wrote(tokv3)
        sl_xb[pbuf].read(tokv3)
        sl_xt[pbuf].wrote(tokv3)
        return tokv3

    def emit_ln_tr(dest_ap, dest_slot, pbuf, ln_slots, psx, xlntm_t, tokv3):
        sl_xb, sl_xt, sl_ps, sl_l2 = ln_slots
        tokt = None
        for c in range(8):
            tokt = em.op("t", lambda t, c=c: t.transpose(
                psx[:, 128 * c:128 * (c + 1)],
                xlntm_t[:, pbuf, 128 * c:128 * (c + 1)], ident[:]),
                waits=(sl_ps.ww() + [tokv3, tok_small] if c == 0 else []),
                inc=(ct if c == 7 else None))
        sl_xt[pbuf].read(tokt)
        sl_ps.wrote(tokt)
        tokev = em.op("v", lambda v: v.tensor_copy(
            dest_ap,
            psx[:].rearrange("p (c f) -> p c f", f=128)),
            waits=[tokt] + dest_slot.ww(), inc=cv)
        sl_ps.read(tokev)
        dest_slot.wrote(tokev)
        return tokev

    def emit_ln(dram_rows, dest_ap, dest_slot, pbuf, ln_slots, tmps, psx, slx,
                xlntm_t, extra_dma_waits=()):
        tokv3 = emit_ln_chain(dram_rows, pbuf, ln_slots, tmps, slx, xlntm_t,
                              extra_dma_waits)
        return emit_ln_tr(dest_ap, dest_slot, pbuf, ln_slots, psx, xlntm_t, tokv3)

    ln_slots = (sl_xbuf, sl_xlntm, sl_pstr, sl_ln)
    tmps = [tuple(lntmp[:, 8 * par + i:8 * par + i + 1] for i in range(7))
            for par in range(3)]

    ln1_pend = []
    for tc in range(RH // 128 + 2):
        if tc < RH // 128:
            tokv3 = emit_ln_chain(xh_d[128 * tc:128 * (tc + 1), :], tc % 3,
                                  ln_slots, tmps, xbuf, xln_tm)
            ln1_pend.append((tc, tokv3))
        if len(ln1_pend) < (2 if tc < RH // 128 else 1):
            continue
        tc, tokv3 = ln1_pend.pop(0)
        emit_ln_tr(xln_dest(tc), sl_xld[tc], tc % 3, ln_slots, ps_tr, xln_tm,
                   tokv3)
        xsrc = xln_dest(tc)
        tokm = None
        for ic in range(8):
            for on in range(2):
                tokm = em.op("t", lambda t, on=on, ic=ic, xsrc=xsrc: t.matmul(
                    ps_p1[:, 512 * on:512 * (on + 1)],
                    xsrc[:, ic, :],
                    wbuf2[:, ic, 512 * on:512 * (on + 1)],
                    start=(ic == 0), stop=False),
                    waits=(sl_p1.ww() + [sl_xld[tc].w, tok_wv]
                           if (on == 0 and ic == 0) else []))
        for on in range(2):
            tokm = em.op("t", lambda t, on=on: t.matmul(
                ps_p1[:, 512 * on:512 * (on + 1)], ones_bf[0:1, 0:128],
                bv_s[0:1, 512 * on:512 * (on + 1)], start=False, stop=True),
                inc=(ct if on == 1 else None))
        sl_p1.wrote(tokm)
        sl_xld[tc].read(tokm)
        toke = em.op("s", lambda s, tc=tc: s.activation(
            v_tm[:, tc, :].rearrange("p (h u) -> p h u", u=65)[:, :, 0:64],
            ps_p1[:].rearrange("p (h u) -> p h u", u=64), AF.Copy),
            waits=[tokm] + sl_vch[tc].ww(), inc=cs)
        sl_p1.read(toke)
        sl_vch[tc].wrote(toke)
        if tc % 4 == 3:
            nt = tc // 4
            blk_deps = [sl_xld[j].w for j in range(4 * nt, 4 * nt + 4)]
            toke_k = None
            for oc in range(8):
                pk = ps_p2 if oc % 2 == 0 else ps_p3
                slk = sl_p2 if oc % 2 == 0 else sl_p3
                half = (oc // 2) % 2
                tokmk = None
                for ic in range(8):
                    em.op("t", lambda t, oc=oc, ic=ic, half=half, nt=nt,
                          pk=pk: t.matmul(
                        pk[:, 512 * half:512 * (half + 1)],
                        wbuf[:, ic, 128 * oc:128 * (oc + 1)],
                        xln_blk_ap(nt, ic),
                        start=(ic == 0), stop=False),
                        waits=(slk.ww() + blk_deps + [tok_wk] if ic == 0 else []))
                tokmk = em.op("t", lambda t, oc=oc, half=half, pk=pk: t.matmul(
                    pk[:, 512 * half:512 * (half + 1)],
                    bk_s[0:1, 128 * oc:128 * (oc + 1)], ones_bf[0:1, 0:512],
                    start=False, stop=True), inc=ct)
                slk.wrote(tokmk)
                toke_k = em.op("s", lambda s, oc=oc, nt=nt, half=half, pk=pk:
                               s.activation(
                    k_fm[:, oc, 512 * nt:512 * (nt + 1)],
                    pk[:, 512 * half:512 * (half + 1)],
                    AF.Copy),
                    waits=[tokmk], inc=cs)
                slk.read(toke_k)
            tok_kblk[nt] = toke_k

    tok_kv_done = (ct, ct.n)

    free_to(attn_mark)
    tri_s = sb("tri_s", [128, 2, 128], BF16)
    q_fm = sb("q_fm", [128, 8, 512], BF16)
    probsT = sb("probsT", [128, 3, 1024], BF16)
    attn_un = sb("attn_un", [128, 512], F32)
    attn_fm = sb("attn_fm", [128, 2, 8, 512], BF16)
    rsum_sb = sb("rsum_sb", [128, 1024], F32)
    x2buf = sb("x2buf", [128, 1, D], F32)
    xbuf = sb("xbuf_w", [128, 2, D], F32)[:].tensor if False else sb("xbuf_w", [128, 1, D], F32)
    ps_scp = [psum("ps_scA", [128, 1024], F32), psum("ps_scB", [128, 1024], F32),
              psum("ps_scC", [128, 1024], F32)]
    ps_pv2 = psum("ps_pv2", [128, 1024], F32)

    bar = [(ct, ct.n), (cs, cs.n), (cv, cv.n)]
    sl_xbuf = [Slot(bar), Slot(bar)]
    sl_q = Slot(bar)
    sl_probs = [Slot(bar), Slot(bar), Slot(bar)]
    sl_attnun = [Slot(bar), Slot(bar)]
    sl_attnfm = [Slot(bar), Slot(bar)]
    sl_rsum = [Slot(bar), Slot(bar)]
    sl_x2buf = [Slot(bar), Slot(bar)]
    sl_wbuf = Slot(bar)
    sl_scp = [Slot(bar), Slot(bar), Slot(bar)]
    sl_pv = [Slot(bar), Slot(bar)]

    tok_tri = gdma(ctri, tri_s[:], tri_d,
                   waits=[tok_kv_done, (cs, cs.n), (cv, cv.n)])
    tok_w = gdma(cwb, wbuf[:], wq_d,
                 waits=[tok_kv_done, (cs, cs.n), (cv, cv.n)])
    sl_wbuf.wrote(tok_w)

    pend_tails = []
    attn_done_tok = [None, None]

    def emit_oproj_group(wo_, tc):
        wpar = wo_ % 2
        pb2 = tc % 2
        rows = xh_d[HALO + 512 * wo_ + 128 * tc: HALO + 512 * wo_ + 128 * (tc + 1), :]
        tokx = ydma(cl[pb2], xbuf[:, 0, :], rows, waits=sl_xbuf[0].ww())
        sl_xbuf[0].wrote(tokx)
        pko = ps_scp[2]
        pslo = sl_scp[2]
        tokmo = None
        for ic in range(8):
            for on in range(2):
                tokmo = em.op("t", lambda t, on=on, ic=ic, tc=tc, pko=pko,
                              wpar=wpar: t.matmul(
                    pko[:, 512 * on:512 * (on + 1)],
                    attn_fm[:, wpar, ic, 128 * tc:128 * (tc + 1)],
                    wbuf[:, ic, 512 * on:512 * (on + 1)],
                    start=(ic == 0), stop=False),
                    waits=(pslo.ww() + [attn_done_tok[wpar], sl_wbuf.w]
                           if (on == 0 and ic == 0) else []))
        for on in range(2):
            tokmo = em.op("t", lambda t, on=on, pko=pko: t.matmul(
                pko[:, 512 * on:512 * (on + 1)], ones_bf[0:1, 0:128],
                bo_s[0:1, 512 * on:512 * (on + 1)], start=False, stop=True),
                inc=(ct if on == 1 else None))
        pslo.wrote(tokmo)
        sl_wbuf.read(tokmo)
        if tc == 3:
            sl_attnfm[wpar].read(tokmo)
        toka = em.op("v", lambda v, pko=pko: v.tensor_add(
            x2buf[:, 0, :], pko[:, :], xbuf[:, 0, :]),
            waits=[tokmo, tokx] + sl_x2buf[0].ww(), inc=cv)
        pslo.read(toka)
        sl_xbuf[0].read(toka)
        sl_x2buf[0].wrote(toka)
        drows = (x2s_d if mode == "full" else out_d)[
            512 * wo_ + 128 * tc: 512 * wo_ + 128 * (tc + 1), :]
        tokst = ydma(cst[pb2], drows, x2buf[:, 0, :], waits=[toka])
        sl_x2buf[0].read(tokst)

    for w in range(NW):
        toke_q = None
        for oc in range(8):
            half = oc % 2
            pq = ps_scp[half]
            slq = sl_scp[half]
            for ic in range(8):
                em.op("t", lambda t, oc=oc, ic=ic, w=w, pq=pq: t.matmul(
                    pq[:, 0:512],
                    wbuf[:, ic, 128 * oc:128 * (oc + 1)],
                    xln_own[:, ic, 512 * w:512 * (w + 1)],
                    start=(ic == 0), stop=False),
                    waits=(slq.ww() + [sl_wbuf.w] if ic == 0 else []))
            tokmq = em.op("t", lambda t, oc=oc, pq=pq: t.matmul(
                pq[:, 0:512],
                bq_s[0:1, 128 * oc:128 * (oc + 1)], ones_bf[0:1, 0:512],
                start=False, stop=True), inc=ct)
            slq.wrote(tokmq)
            toke_q = em.op("v", lambda v, oc=oc, pq=pq: v.tensor_copy(
                q_fm[:, oc, :], pq[:, 0:512]),
                waits=[tokmq] + (sl_q.ww() if oc == 0 else []), inc=cv)
            slq.read(toke_q)
            sl_wbuf.read(tokmq)
        sl_q.wrote(toke_q)

        tok_w = gdma(cwb, wbuf[:], wo_d, waits=sl_wbuf.ww())
        sl_wbuf.wrote(tok_w)

        CORDER = [4, 5, 6, 7, 12, 0, 1, 2, 3, 8, 9, 10, 11]
        for hc in range(H // 2):
            exp_toks = [None] * 13
            sc_toks = [None] * 13

            def emit_scores(ci, pb, hc=hc, w=w, sc_toks=sc_toks):
                jc = CORDER[ci]
                pps = ps_scp[pb]
                off, wid = _band(jc)
                if jc < 12:
                    kof = 512 * w + 128 * jc
                    kdep = tok_kblk[kof // 512]
                else:
                    kdep = t_gm
                tokm = None
                for hp in range(2):
                    hoff = 64 * hp
                    if jc < 12:
                        klhs = k_fm[hoff:hoff + 64, hc, kof:kof + 128]
                    else:
                        klhs = kpad[hoff:hoff + 64, :]
                    tokm = em.op("t", lambda t, klhs=klhs, hp=hp, pps=pps,
                                 hoff=hoff, hc=hc, off=off, wid=wid: t.matmul(
                        pps[:, 512 * hp + off:512 * hp + off + wid],
                        klhs, q_fm[hoff:hoff + 64, hc, off:off + wid],
                        start=True, stop=True),
                        waits=(sl_scp[pb].ww() + [sl_q.w, kdep]) if hp == 0 else [],
                        inc=(ct if hp == 1 else None))
                sl_scp[pb].wrote(tokm)
                sl_q.read(tokm)
                sc_toks[ci] = tokm

            def emit_exp(ci, pb, w=w, sc_toks=sc_toks, exp_toks=exp_toks):
                jc = CORDER[ci]
                pps = ps_scp[pb]
                off, wid = _band(jc)
                ws_ = [sc_toks[ci], tok_small] + sl_probs[pb].ww()
                toke = None
                if wid == 512:
                    toke = em.op("s", lambda s, pb=pb, pps=pps, jc=jc, w=w:
                                 s.activation(
                        probsT[:, pb, 0:1024], pps[:, 0:1024],
                        AF.Exp, bias=mb_s[:, w * NJC + jc:w * NJC + jc + 1]),
                        waits=ws_, inc=cs)
                else:
                    for hp in range(2):
                        lo = 512 * hp + off
                        toke = em.op("s", lambda s, pb=pb, pps=pps, jc=jc, w=w,
                                     lo=lo, wid=wid: s.activation(
                            probsT[:, pb, lo:lo + wid], pps[:, lo:lo + wid],
                            AF.Exp, bias=mb_s[:, w * NJC + jc:w * NJC + jc + 1]),
                            waits=ws_, inc=cs)
                        ws_ = []
                sl_scp[pb].read(toke)
                sl_probs[pb].wrote(toke)
                ptok = toke
                if jc < 4:
                    tlo, tt = 128 * jc, 0
                elif 8 <= jc < 12:
                    tlo, tt = 128 * (jc - 8), 1
                else:
                    tlo = None
                if tlo is not None:
                    for hp in range(2):
                        lo = 512 * hp + tlo
                        ptok = em.op("v", lambda v, pb=pb, lo=lo, tt=tt:
                                     v.tensor_mul(
                            probsT[:, pb, lo:lo + 128], probsT[:, pb, lo:lo + 128],
                            tri_s[:, tt, :]),
                            waits=[toke, tok_tri], inc=cv)
                    sl_probs[pb].wrote(ptok)
                exp_toks[ci] = ptok

            def emit_pv(ci, pb, first_pv, last_pv, hc=hc, w=w,
                        exp_toks=exp_toks):
                jc = CORDER[ci]
                off, wid = _band(jc)
                tokpv = None
                for hp in range(2):
                    h = 2 * hc + hp
                    if jc < 12:
                        vl = v_tm[:, 4 * w + jc, :].rearrange(
                            "p (h u) -> p h u", u=65)[:, h, :]
                        vdep = sl_vch[4 * w + jc].w
                    else:
                        vl = vpad[:].rearrange("p (h u) -> p h u", u=65)[:, h, :]
                        vdep = t_vones
                    pvlo = 512 * hp
                    tokpv = em.op("t", lambda t, vl=vl, pb=pb, hp=hp, pvlo=pvlo,
                                  off=off, wid=wid, st_=first_pv, sp_=last_pv:
                                  t.matmul(
                        ps_pv2[0:65, pvlo + off:pvlo + off + wid],
                        vl, probsT[:, pb, 512 * hp + off:512 * hp + off + wid],
                        start=st_, stop=sp_),
                        waits=[exp_toks[ci], vdep]
                        + (sl_pv[hp].ww() if first_pv else []),
                        inc=(ct if (last_pv and hp == 1) else None))
                return tokpv

            for i0 in range(3):
                emit_scores(i0, i0 % 3)
                emit_exp(i0, i0 % 3)
                if pend_tails:
                    pend_tails.pop(0)()
            while pend_tails:
                pend_tails.pop(0)()
            for i in range(13):
                tokpv = emit_pv(i, i % 3, first_pv=(i == 0), last_pv=(i == 12))
                if i + 3 < 13:
                    emit_scores(i + 3, (i + 3) % 3)
                    emit_exp(i + 3, (i + 3) % 3)
            sl_pv[0].wrote(tokpv)
            sl_pv[1].wrote(tokpv)
            for sp in sl_probs:
                sp.read(tokpv)
            for hp in range(2):
                pvlo = 512 * hp
                hoff = 64 * hp
                tokr = em.op("v", lambda v, pvlo=pvlo: v.reciprocal(
                    rsum_sb[64:65, pvlo:pvlo + 512],
                    ps_pv2[64:65, pvlo:pvlo + 512]),
                    waits=[tokpv] + sl_rsum[hp].ww(), inc=cv)
                sl_rsum[hp].wrote(tokr)
                toke2 = em.op("s", lambda s, pvlo=pvlo, hoff=hoff: s.activation(
                    attn_un[hoff:hoff + 64, :], ps_pv2[0:64, pvlo:pvlo + 512],
                    AF.Copy),
                    waits=[tokpv] + sl_attnun[hp].ww(), inc=cs)
                sl_attnun[hp].wrote(toke2)
                sl_pv[hp].read(tokr)
                sl_pv[hp].read(toke2)

                def mk_tail(hp=hp, tokr=tokr, toke2=toke2, hc=hc, wpar=w % 2):
                    def run():
                        hoff = 64 * hp
                        pvlo = 512 * hp
                        tokb = em.op("t", lambda t: t.matmul(
                            ps_pv2[:, pvlo:pvlo + 512], ones_f[64:65, :],
                            rsum_sb[64:65, pvlo:pvlo + 512],
                            start=True, stop=True),
                            waits=[tokr, toke2, t_gm] + sl_pv[hp].ww(), inc=ct)
                        sl_rsum[hp].read(tokb)
                        sl_pv[hp].wrote(tokb)
                        tokn = em.op("v", lambda v: v.tensor_mul(
                            attn_fm[hoff:hoff + 64, wpar, hc, :],
                            attn_un[hoff:hoff + 64, :],
                            ps_pv2[hoff:hoff + 64, pvlo:pvlo + 512]),
                            waits=[tokb, toke2]
                            + (sl_attnfm[wpar].ww() if hc == 0 else []), inc=cv)
                        sl_attnun[hp].read(tokn)
                        sl_pv[hp].read(tokn)
                        if hc == H // 2 - 1 and hp == 1:
                            sl_attnfm[wpar].wrote(tokn)
                            attn_done_tok[wpar] = tokn
                    return run
                pend_tails.append(mk_tail())
            if w >= 1 and hc in (1, 3, 5, 7):
                emit_oproj_group(w - 1, (hc - 1) // 2)

        if w < NW - 1:
            tok_w = gdma(cwb, wbuf[:], wq_d, waits=sl_wbuf.ww())
            sl_wbuf.wrote(tok_w)

    while pend_tails:
        pend_tails.pop(0)()
    for tc in range(4):
        emit_oproj_group(NW - 1, tc)

    if mode == "attn":
        _finalize(nc, em, ctrs, [(cst[0], cst[0].n), (cst[1], cst[1].n)])
        return nc

    free_to(persist_mark)
    bar = [(ct, ct.n), (cs, cs.n), (cv, cv.n),
           (cst[0], cst[0].n), (cst[1], cst[1].n),
           (cl[0], cl[0].n), (cl[1], cl[1].n), (cwb, cwb.n), (ctri, ctri.n)]

    ps_tr2 = psum("ps_tr2", [128, 1024], BF16)
    ps_fa = psum("ps_fa", [128, 1024], F32)
    ps_fb = psum("ps_fb", [128, 1024], F32)

    x2in = sb("x2in", [128, 2, D], F32)
    lntmp2 = sb("lntmp2", [128, 16], F32)
    xln2_tm = sb("xln2_tm", [128, 2, D], BF16)
    xln2_fm = sb("xln2_fm", [128, 8, TP], BF16)
    y_fm = sb("y_fm", [128, 8, TP], BF16)
    a_fm = sb("a_fm", [128, 32, TP], BF16)
    b_fm = sb("b_fm", [128, 32, TP], BF16)
    hcol = sb("hcol", [128, 2, 4096], BF16)
    t1_sb = sb("t1_sb", [128, TP], BF16)
    osb = sb("osb", [128, 1, D], F32)

    tmps2 = [tuple(lntmp2[:, 8 * par + i:8 * par + i + 1] for i in range(7))
             for par in range(2)]

    sl_x2in = [Slot(bar), Slot(bar)]
    sl_x2tm = [Slot(bar), Slot(bar)]
    sl_ln2 = [Slot(bar), Slot(bar)]
    sl_ps2 = Slot(bar)
    sl_xln2 = Slot(bar)
    sl_afm = Slot(bar)
    sl_bfm = Slot(bar)
    sl_hcol = [Slot(bar), Slot(bar)]
    sl_t1 = Slot(bar)
    sl_fa = Slot(bar)
    sl_fb = Slot(bar)
    sl_osb = [Slot(bar), Slot(bar)]
    sl_yfm = Slot(bar)
    ln2_slots = (sl_x2in, sl_x2tm, sl_ps2, sl_ln2)

    x2_stores = [(cst[0], cst[0].n), (cst[1], cst[1].n)]

    def emit_ln2(p):
        tokev = None
        pend = []
        for tc in range(TP // 128 + 2):
            if tc < TP // 128:
                row0 = TP * p + 128 * tc
                tokv3 = emit_ln_chain(x2s_d[row0:row0 + 128, :], tc % 2,
                                      ln2_slots, tmps2, x2in, xln2_tm,
                                      extra_dma_waits=x2_stores)
                pend.append((tc, tokv3))
            if len(pend) < (2 if tc < TP // 128 else 1):
                continue
            tcp, tokv3 = pend.pop(0)
            tokev = emit_ln_tr(xln2_fm[:, :, 128 * tcp:128 * (tcp + 1)],
                               sl_xln2 if tcp == TP // 128 - 1 else Slot(),
                               tcp % 2, ln2_slots, ps_tr2, xln2_tm, tokv3)
        sl_xln2.wrote(tokev)

    emit_ln2(0)
    for p in range(NPASS):
        tokav = None
        for oc in range(32):
            hb = oc % 2
            tokw1 = gdma(ch[hb], hcol[:, hb, 0:1024], w1_d[oc],
                         waits=sl_hcol[hb].ww())
            tokw2 = gdma(ch[hb], hcol[:, hb, 1024:2048], w2_d[oc])
            sl_hcol[hb].wrote(tokw2)
            tokm1 = None
            for nt in range(TP // 512):
                for ic in range(8):
                    tokm1 = em.op("t", lambda t, ic=ic, nt=nt, hb=hb: t.matmul(
                        ps_fa[:, 512 * nt:512 * (nt + 1)],
                        hcol[:, hb, 128 * ic:128 * (ic + 1)],
                        xln2_fm[:, ic, 512 * nt:512 * (nt + 1)],
                        start=(ic == 0), stop=(ic == 7)),
                        waits=(sl_fa.ww() + [tokw2, sl_xln2.w]
                               if (nt == 0 and ic == 0) else []),
                        inc=(ct if (nt == TP // 512 - 1 and ic == 7) else None))
            sl_fa.wrote(tokm1)
            tokm2 = None
            for nt in range(TP // 512):
                for ic in range(8):
                    tokm2 = em.op("t", lambda t, ic=ic, nt=nt, hb=hb: t.matmul(
                        ps_fb[:, 512 * nt:512 * (nt + 1)],
                        hcol[:, hb, 1024 + 128 * ic:1024 + 128 * (ic + 1)],
                        xln2_fm[:, ic, 512 * nt:512 * (nt + 1)],
                        start=(ic == 0), stop=(ic == 7)),
                        waits=(sl_fb.ww() if (nt == 0 and ic == 0) else []),
                        inc=(ct if (nt == TP // 512 - 1 and ic == 7) else None))
            sl_fb.wrote(tokm2)
            sl_hcol[hb].read(tokm2)
            sl_xln2.read(tokm2)
            toksl = em.op("s", lambda s, oc=oc: s.activation(
                t1_sb[:], ps_fa[:, :], AF.Silu, scale=s1_s[:, oc:oc + 1]),
                waits=[tokm1] + sl_t1.ww(), inc=cs)
            sl_fa.read(toksl)
            sl_t1.wrote(toksl)
            tokav = em.op("v", lambda v, oc=oc: v.scalar_tensor_tensor(
                a_fm[:, oc, :], ps_fb[:, :], s2_s[:, oc:oc + 1], t1_sb[:],
                op0=ALU.mult, op1=ALU.mult),
                waits=[tokm2, toksl] + (sl_afm.ww() if oc == 0 else []), inc=cv)
            sl_fb.read(tokav)
            sl_t1.read(tokav)
        sl_afm.wrote(tokav)

        def big_stage(src_fm, src_slot, dst_fm, dst_slot, w_dram, sc_ap, nob, act,
                      nic, filler=None):
            tokl = None
            for oc in range(nob):
                if filler is not None:
                    filler(oc)
                hb = oc % 2
                pso = ps_fa if oc % 2 == 0 else ps_fb
                slo = sl_fa if oc % 2 == 0 else sl_fb
                tokw = gdma(ch[hb], hcol[:, hb, 0:128 * nic],
                            w_dram[oc][:, 0:128 * nic], waits=sl_hcol[hb].ww())
                sl_hcol[hb].wrote(tokw)
                tokm = None
                for nt in range(TP // 512):
                    for ic in range(nic):
                        tokm = em.op("t", lambda t, ic=ic, nt=nt, hb=hb, pso=pso:
                                     t.matmul(
                            pso[:, 512 * nt:512 * (nt + 1)],
                            hcol[:, hb, 128 * ic:128 * (ic + 1)],
                            src_fm[:, ic, 512 * nt:512 * (nt + 1)],
                            start=(ic == 0), stop=(ic == nic - 1)),
                            waits=(slo.ww() + [tokw, src_slot.w]
                                   if (nt == 0 and ic == 0) else []),
                            inc=(ct if (nt == TP // 512 - 1 and ic == nic - 1)
                                 else None))
                slo.wrote(tokm)
                sl_hcol[hb].read(tokm)
                src_slot.read(tokm)
                tokl = em.op("s", lambda s, oc=oc, pso=pso: s.activation(
                    dst_fm[:, oc, :], pso[:, :], act, scale=sc_ap[:, oc:oc + 1]),
                    waits=[tokm] + (dst_slot.ww() if oc == 0 else []), inc=cs)
                slo.read(tokl)
            dst_slot.wrote(tokl)
            return tokl

        if p + 1 < NPASS:
            ln2_state = [0, None]

            def ln2_filler(oc, p=p):
                tc_n, pend = ln2_state
                if oc % 2 == 1 and pend is not None:
                    tc, tokv3 = pend
                    tokev = emit_ln_tr(xln2_fm[:, :, 128 * tc:128 * (tc + 1)],
                                       sl_xln2 if tc == TP // 128 - 1 else Slot(),
                                       tc % 2, ln2_slots, ps_tr2, xln2_tm, tokv3)
                    if tc == TP // 128 - 1:
                        sl_xln2.wrote(tokev)
                    ln2_state[1] = None
                elif oc % 2 == 0 and tc_n < TP // 128 and pend is None:
                    row0 = TP * (p + 1) + 128 * tc_n
                    tokv3 = emit_ln_chain(x2s_d[row0:row0 + 128, :], tc_n % 2,
                                          ln2_slots, tmps2, x2in, xln2_tm,
                                          extra_dma_waits=x2_stores)
                    ln2_state[0] = tc_n + 1
                    ln2_state[1] = (tc_n, tokv3)
        else:
            ln2_filler = None
        big_stage(a_fm, sl_afm, b_fm, sl_bfm, h1_d, sh1_s, 32, AF.Gelu, 32,
                  filler=ln2_filler)
        big_stage(b_fm, sl_bfm, a_fm, sl_afm, h2_d, sh2_s, 32, AF.Gelu, 32)
        toky = big_stage(a_fm, sl_afm, y_fm, sl_yfm, w3_d, s3_s, 8, AF.Copy, 32)

        for tc in range(TP // 128):
            row0 = TP * p + 128 * tc
            pb2 = tc % 2
            tokx = ydma(cl[pb2], x2in[:, pb2, :], x2s_d[row0:row0 + 128, :],
                        waits=sl_x2in[pb2].ww() + x2_stores)
            sl_x2in[pb2].wrote(tokx)
            tokt = None
            for c in range(8):
                tokt = em.op("t", lambda t, c=c, tc=tc: t.transpose(
                    ps_tr2[:, 128 * c:128 * (c + 1)],
                    y_fm[:, c, 128 * tc:128 * (tc + 1)], ident[:]),
                    waits=(sl_ps2.ww() + [toky] if c == 0 else []),
                    inc=(ct if c == 7 else None))
            sl_ps2.wrote(tokt)
            sl_yfm.read(tokt)
            toko = em.op("v", lambda v, pb2=pb2: v.tensor_add(
                osb[:, 0, :], ps_tr2[:, :], x2in[:, pb2, :]),
                waits=[tokt, tokx] + sl_osb[0].ww(), inc=cv)
            sl_ps2.read(toko)
            sl_x2in[pb2].read(toko)
            sl_osb[0].wrote(toko)
            tokst = ydma(cst[pb2], out_d[row0:row0 + 128, :], osb[:, 0, :],
                         waits=[toko])
            sl_osb[0].read(tokst)

    _finalize(nc, em, ctrs, [(cst[0], cst[0].n), (cst[1], cst[1].n)])
    return nc


def _finalize(nc, em, ctrs, final_tok):
    import contextlib
    with contextlib.ExitStack() as st:
        sems = {c.name: st.enter_context(nc.semaphore(c.name)) for c in ctrs}
        with nc.Block() as block:

            @block.tensor
            def _(eng):
                em.replay("t", eng, sems)

            @block.scalar
            def _(eng):
                em.replay("s", eng, sems)

            @block.vector
            def _(eng):
                em.replay("v", eng, sems)

            @block.gpsimd
            def _(eng):
                em.replay("g", eng, sems)

            @block.sync
            def _(eng):
                em.replay("y", eng, sems)
                for c, v in final_tok:
                    eng.wait_ge(sems[c.name], v)


def _prep_shared(L0_attn_wq, L0_attn_bq, L0_attn_wk, L0_attn_bk, L0_attn_wv,
                 L0_attn_bv, L0_attn_wo, L0_attn_bo, L0_ffn_w1_w, L0_ffn_w1_s,
                 L0_ffn_w2_w, L0_ffn_w2_s, L0_ffn_h1_w, L0_ffn_h1_s, L0_ffn_h2_w,
                 L0_ffn_h2_s, L0_ffn_w3_w, L0_ffn_w3_s):
    f = np.float32
    c = np.ascontiguousarray

    def wt_qkvo(w):
        return c(np.asarray(w, f).T.reshape(8, 128, D).transpose(1, 0, 2))

    def tile_in(wT, nic, noc):
        t = np.asarray(wT, f).reshape(nic, 128, noc, 128).transpose(2, 1, 0, 3)
        return c(t.reshape(noc, 128, nic * 128))

    d = {
        "wqT": wt_qkvo(L0_attn_wq), "wkT": wt_qkvo(L0_attn_wk),
        "wvT": wt_qkvo(L0_attn_wv), "woT": wt_qkvo(L0_attn_wo),
        "w1t": tile_in(np.asarray(L0_ffn_w1_w, f).T, 8, 32),
        "w2t": tile_in(np.asarray(L0_ffn_w2_w, f).T, 8, 32),
        "h1t": tile_in(np.asarray(L0_ffn_h1_w, f).T, 32, 32),
        "h2t": tile_in(np.asarray(L0_ffn_h2_w, f).T, 32, 32),
        "w3t": tile_in(np.asarray(L0_ffn_w3_w, f).T, 32, 8),
        "bqr": np.asarray(L0_attn_bq, f).reshape(1, D).copy(),
        "bkr": np.asarray(L0_attn_bk, f).reshape(1, D).copy(),
        "bvr": np.asarray(L0_attn_bv, f).reshape(1, D).copy(),
        "bor": np.asarray(L0_attn_bo, f).reshape(1, D).copy(),
        "s1v": c(np.asarray(L0_ffn_w1_s, f).reshape(32, 128).T),
        "s2v": c(np.asarray(L0_ffn_w2_s, f).reshape(32, 128).T),
        "sh1v": c(np.asarray(L0_ffn_h1_s, f).reshape(32, 128).T),
        "sh2v": c(np.asarray(L0_ffn_h2_s, f).reshape(32, 128).T),
        "s3v": c(np.asarray(L0_ffn_w3_s, f).reshape(8, 128).T),
        "ident": np.eye(128, dtype=f),
    }
    pp = np.arange(128)[:, None]
    ii = np.arange(128)[None, :]
    tri = np.zeros((128, 2, 128), f)
    tri[:, 0, :] = (ii <= pp)
    tri[:, 1, :] = (ii >= pp)
    d["tri01"] = tri
    return d


def _prep_core(x, core):
    f = np.float32
    b = core // 4
    T0 = (core % 4) * R
    xh = np.zeros((RH, D), f)
    lo = max(0, T0 - HALO)
    hi = min(S_SEQ, T0 + R + HALO)
    xh[lo - (T0 - HALO): hi - (T0 - HALO), :] = np.asarray(x[b, lo:hi, :], f)
    mb = np.empty((128, NW * NJC), f)
    for w in range(NW):
        for jc in range(12):
            g = T0 - HALO + 512 * w + 128 * jc + np.arange(128)
            real = (g >= 0) & (g < S_SEQ)
            mb[:, w * NJC + jc] = np.where(real, SHIFT, SHIFT + MASKV)
        edge = (T0 + 512 * w == 0) or (T0 + 512 * w + 512 == S_SEQ)
        col = np.full(128, SHIFT + MASKV, f)
        if edge:
            col[0] = SHIFT
        mb[:, w * NJC + 12] = col
    return {"xh": xh, "maskb": mb}


def kernel(**inputs):
    mode = inputs.pop("_mode", "full")
    x = np.asarray(inputs.pop("L0_attn_x"), np.float32)
    shared = _prep_shared(**inputs)
    in_maps = []
    for core in range(N_CORES):
        m = dict(shared)
        m.update(_prep_core(x, core))
        in_maps.append(m)
    nc = build(mode)
    res = run_bass_kernel_spmd(nc, in_maps, core_ids=list(range(N_CORES)))
    parts = [res.results[core]["out"] for core in range(N_CORES)]
    out = np.empty((B_BATCH, S_SEQ, D), np.float32)
    for core in range(N_CORES):
        b = core // 4
        T0 = (core % 4) * R
        out[b, T0:T0 + R, :] = parts[core]
    return out
